# version 29
# baseline (speedup 1.0000x reference)
import numpy as np
import ml_dtypes
import concourse.bass as bass
import concourse.mybir as mybir
from concourse.bass_utils import run_bass_kernel_spmd
from contextlib import ExitStack

F32 = mybir.dt.float32
BF16 = mybir.dt.bfloat16
AF = mybir.ActivationFunctionType
ALU = mybir.AluOpType
AX = mybir.AxisListType.X

D, DI, DS, FF = 1024, 2048, 64, 4096
TPC, HALO = 2048, 3
NCH, CH = 4, 512
NECH, ECH = 4, 512
EPS = 1e-5

THS = (0.24419428096993973, 0.46396913384288541, 0.68618592952553059,
       0.90596078239847633, 1.0839784132255623)

ENGS = ("sync", "scalar", "vector", "tensor", "gpsimd")
DMAQ = ("sync", "scalar", "vector", "gpsimd")


class _Op:
    __slots__ = ("eng", "fn", "waits", "hard", "inc", "val", "need", "dma")

    def __init__(self, eng, fn, waits, hard, dma):
        self.eng, self.fn = eng, fn
        self.dma = dma
        self.inc = 16 if dma else 1
        self.waits = [w for w in waits if w is not None]
        self.hard = [w for w in hard if w is not None]
        self.val = None
        self.need = False


class Sched:
    def __init__(self):
        self.prog = {e: [] for e in ENGS}

    def add(self, eng, fn, waits=(), hard=(), dma=False):
        op = _Op(eng, fn, waits, hard, dma)
        self.prog[eng].append(op)
        return op

    def finalize(self):
        for e in ENGS:
            for op in self.prog[e]:
                for w in op.waits:
                    w.need = True
                for w in op.hard:
                    w.need = True
        for e in ENGS:
            cc, cd = 0, 0
            for op in self.prog[e]:
                if op.dma:
                    cd += op.inc
                    op.val = cd
                elif op.need:
                    cc += op.inc
                    op.val = cc

    def emitter(self, e, sems):
        def run(eng):
            waited = {}
            last_dma = None
            for op in self.prog[e]:
                if op.dma and last_dma is not None and \
                        waited.get((e, 'd'), -1) < last_dma:
                    eng.wait_ge(sems[(e, 'd')], last_dma)
                    waited[(e, 'd')] = last_dma
                for w in op.waits:
                    k = (w.eng, 'd' if w.dma else 'c')
                    if waited.get(k, -1) >= w.val:
                        continue
                    eng.wait_ge(sems[k], w.val)
                    waited[k] = w.val
                for w in op.hard:
                    k = (w.eng, 'd' if w.dma else 'c')
                    if waited.get(k, -1) >= w.val:
                        continue
                    eng.wait_ge(sems[k], w.val)
                    waited[k] = w.val
                ins = op.fn(eng)
                if op.val is not None:
                    ins.then_inc(sems[(e, 'd' if op.dma else 'c')], op.inc)
                    if op.dma:
                        last_dma = op.val
        return run


def _build(debug=False):
    nc = bass.Bass("TRN2", target_bir_lowering=False)
    S = Sched()

    x_in = nc.dram_tensor("x_in", [HALO + TPC, D], F32, kind="ExternalInput")
    w_in = nc.dram_tensor("w_in", [D, 2 * DI], F32, kind="ExternalInput")
    w_dt = nc.dram_tensor("w_dt", [DI, DI], F32, kind="ExternalInput")
    w_x = nc.dram_tensor("w_x", [DI, 2 * DS], F32, kind="ExternalInput")
    w_out = nc.dram_tensor("w_out", [DI, D], F32, kind="ExternalInput")
    w_fc = nc.dram_tensor("w_fc", [FF, D], F32, kind="ExternalInput")
    w_pr = nc.dram_tensor("w_pr", [D, FF], F32, kind="ExternalInput")
    lngT = nc.dram_tensor("lngT", [128, 8], F32, kind="ExternalInput")
    lnbT = nc.dram_tensor("lnbT", [128, 8], F32, kind="ExternalInput")
    convw = nc.dram_tensor("convw", [128, 64], F32, kind="ExternalInput")
    convb = nc.dram_tensor("convb", [128, 16], F32, kind="ExternalInput")
    dvec = nc.dram_tensor("dvec", [128, 16], F32, kind="ExternalInput")
    bdt = nc.dram_tensor("bdt", [128, 16], F32, kind="ExternalInput")
    alog = nc.dram_tensor("alog", [DS, 1], F32, kind="ExternalInput")
    selv = nc.dram_tensor("selv", [DS, 8], F32, kind="ExternalInput")
    maskv = nc.dram_tensor("maskv", [128, 1], F32, kind="ExternalInput")
    identf = nc.dram_tensor("identf", [128, 128], F32, kind="ExternalInput")
    out_t = nc.dram_tensor("out", [TPC, D], F32, kind="ExternalOutput")

    sc_u = nc.dram_tensor("sc_u", [128, 16, TPC], F32)
    sc_sres = nc.dram_tensor("sc_sres", [128, 16, TPC], F32)
    sc_cnt = nc.dram_tensor("sc_cnt", [TPC, D], BF16)
    wfc_bf = nc.dram_tensor("wfc_bf", [FF, D], BF16)
    wpr_bf = nc.dram_tensor("wpr_bf", [D, FF], BF16)
    wout_d = nc.dram_tensor("wout_d", [DI, D], BF16)
    sc_dl = nc.dram_tensor("sc_dl", [1, TPC], F32)
    sc_ys = nc.dram_tensor("sc_ys", [1, TPC], F32)
    cc_in = nc.dram_tensor("cc_in", [1, 2 * DS], F32)
    cc_out = nc.dram_tensor("cc_out", [8, 2 * DS], F32)

    dbg = {}
    if debug:
        for nm, shp in (("u", [128, 16, TPC]),
                        ("sres", [128, 16, TPC]), ("cnt_", [TPC, D]),
                        ("bc", [128, TPC]), ("delta", [1, TPC]),
                        ("ys", [1, TPC]), ("hgt", [2, DS]),
                        ("ad", [DS, TPC]), ("hs1", [DS, TPC]),
                        ("hs2", [DS, TPC]), ("cy", [128, TPC // 2])):
            dt = BF16 if nm == "cnt_" else F32
            dbg[nm] = nc.dram_tensor("dbg_" + nm, shp, dt, kind="ExternalOutput")

    with ExitStack() as ctx:
        E = ctx.enter_context

        G = [E(nc.sbuf_tensor(f"g{i}", [128, 2048], F32)) for i in range(25)]

        def gh(i, h):
            return G[i][:, h * 1024:(h + 1) * 1024]

        def gq(i, q):
            return G[i][:, q * 512:(q + 1) * 512]

        def gbf(i):
            return G[i][:].bitcast(BF16)

        wx_sb = E(nc.sbuf_tensor("wx_sb", [128, 16 * 128], BF16))
        ct = E(nc.sbuf_tensor("ct", [128, 8], F32))
        ones_c = E(nc.sbuf_tensor("ones_c", [128, 1], BF16))
        bdtc = E(nc.sbuf_tensor("bdtc", [128, 16], F32))
        idf = E(nc.sbuf_tensor("idf", [64, 64], F32))
        idb = E(nc.sbuf_tensor("idb", [128, 128], BF16))
        cw = E(nc.sbuf_tensor("cw", [128, 64], F32))
        cb = E(nc.sbuf_tensor("cb", [128, 16], F32))
        dv = E(nc.sbuf_tensor("dv", [128, 16], F32))
        msk = E(nc.sbuf_tensor("msk", [128, 1], F32))
        xn2h = E(nc.sbuf_tensor("xn2h", [128, 8 * HALO], BF16))
        hist = E(nc.sbuf_tensor("hist", [128, 48], F32))
        a_col = E(nc.sbuf_tensor("a_col", [DS, 1], F32))
        pend = E(nc.sbuf_tensor("pend", [DS, 8], F32))
        hg = E(nc.sbuf_tensor("hg", [DS, 2], F32))
        hgT = E(nc.sbuf_tensor("hgT", [2, DS], F32))
        ag = E(nc.sbuf_tensor("ag", [8, 2 * DS], F32))
        agH = E(nc.sbuf_tensor("agH", [DS, 8], F32))
        agG = E(nc.sbuf_tensor("agG", [DS, 8], F32))
        dtile = E(nc.sbuf_tensor("dtile", [DS, 8], F32))
        dcur = E(nc.sbuf_tensor("dcur", [DS, 1], F32))
        hin = E(nc.sbuf_tensor("hin", [DS, 1], F32))
        sel_sb = E(nc.sbuf_tensor("sel_sb", [DS, 8], F32))
        astat = E(nc.sbuf_tensor("astat", [128, 64], F32))
        pstat = E(nc.sbuf_tensor("pstat", [128, 51], F32))
        bvb = E(nc.sbuf_tensor("bvb", [128, 16], F32))
        bva = E(nc.sbuf_tensor("bva", [128, 16], F32))
        lngc = E(nc.sbuf_tensor("lngc", [128, 8], F32))
        lnbc = E(nc.sbuf_tensor("lnbc", [128, 8], F32))
        lnb_bf = E(nc.sbuf_tensor("lnb_bf", [128, 8], BF16))

        PS = [E(nc.psum_tensor(f"ps{i}", [128, 512], F32)) for i in range(8)]
        PSB = [PS[6][:].bitcast(BF16), PS[7][:].bitcast(BF16)]

        sems = {}
        for e in ENGS:
            sems[(e, 'c')] = E(nc.semaphore("c_" + e))
        for e in DMAQ:
            sems[(e, 'd')] = E(nc.semaphore("d_" + e))

        xn2f = [gh(i // 2, i % 2).bitcast(BF16) for i in range(8)]
        xnbf = xn2f
        winA = [gh(4 + i // 2, i % 2).bitcast(BF16) for i in range(8)]
        winB = [gh(8 + i // 2, i % 2).bitcast(BF16) for i in range(8)]
        wdt_sb = [gh(12 + i // 2, i % 2).bitcast(BF16) for i in range(16)]
        wfcT = [gbf(4 + i) for i in range(8)]
        wprT = [gbf(12 + i)[:, j * 1024:(j + 1) * 1024]
                for i in range(8) for j in range(4)]
        xcbf = [gbf(20)[:, i * 512:(i + 1) * 512] for i in range(8)] + \
               [gbf(21)[:, i * 512:(i + 1) * 512] for i in range(8)]
        htile = [gbf(20 + i // 8)[:, (i % 8) * 512:(i % 8 + 1) * 512]
                 for i in range(32)]
        y2t2 = [[gbf(4)[:, i * 512:(i + 1) * 512] for i in range(8)] +
                [gbf(5)[:, i * 512:(i + 1) * 512] for i in range(8)],
                xcbf]
        bst_bf = G[22][:].bitcast(BF16)[0:64, 0:2048]
        cst_bf = G[22][:].bitcast(BF16)[0:64, 2048:4096]
        ysb = G[10]
        ad_sb = G[24][0:64, :]
        hsl = G[23][0:64, :]
        cy_bf = G[24][:].bitcast(BF16)[0:64, 0:2048]
        wout_sb = [gbf(6 + i // 4)[:, (i % 4) * 1024:(i % 4 + 1) * 1024]
                   for i in range(16)]
        cntT = [G[22][:].bitcast(BF16)[:, i * 1024:(i + 1) * 1024]
                for i in range(4)]
        rl_b = [gq(24, 0), gq(24, 1)]
        cnt_b = [gq(24, 2), gq(24, 3)]
        xp_b = [G[23][:, 0:1024], G[23][:, 1533:2048]]
        acc_b = [gq(24, 0), gq(24, 1)]
        xcf_b = [gq(24, 2), gq(24, 3)]
        u_b = acc_b
        sp_f = acc_b[0]
        sp_g2 = [acc_b[1].bitcast(BF16)[:, 0:512],
                 acc_b[1].bitcast(BF16)[:, 512:1024]]
        sre_b = [gq(22, j) for j in range(4)] + \
            [gq(23, j) for j in range(4)]
        fub = [G[0], G[1]]
        fsrb = [G[2], G[3]]
        f_xn = [gh(16, 0), gh(18, 0)]
        f_x4 = [gh(16, 1), gh(18, 1), gh(14, 0), gh(14, 1)]
        f_s4 = [gh(17, 0), gh(12, 0), gh(15, 0), gh(15, 1)]
        f_g2 = [gh(17, 1), gh(12, 1)]
        f_sg2 = [gh(19, 0), gh(13, 0)]
        f_o2 = [gh(19, 1), gh(13, 1)]
        f_t9 = [gq(24, 0), gq(24, 1)]
        dtmp = G[23][0:64, 0:512]
        drow = G[23][0:1, 515:1027]
        dsel = pend

        def dma(q, out, in_, waits=(), **kw):
            return S.add(q, lambda e, o=out, i=in_, k=kw: e.dma_start(o, i, **k),
                         waits=waits, dma=True)

        def op(eng, fn, waits=(), hard=()):
            return S.add(eng, fn, waits=waits, hard=hard)

        c_eps = ct[:, 0:1]
        c_one = ct[:, 1:2]
        c_mh = ct[:, 2:3]
        c_p2 = ct[:, 4:5]

        t_ct = op("vector", lambda e: e.memset(ct[:, 0:8], 0.0))
        for col, val in ((0, EPS), (1, 1.0), (2, -0.5), (4, 0.2)):
            t_ct = op("vector", lambda e, c=col, v=val: e.memset(ct[:, c:c + 1], v),
                      waits=(t_ct,))
        t_onec = op("vector", lambda e: e.memset(ones_c[:, :], 1.0), waits=(t_ct,))

        ld = {}
        for nm, dst, src in (("cw", cw, convw), ("cb", cb, convb), ("dv", dv, dvec),
                             ("msk", msk, maskv),
                             ("alog", a_col, alog), ("sel", sel_sb, selv),
                             ("bdt", bdtc, bdt), ("lng", lngc, lngT),
                             ("lnb", lnbc, lnbT)):
            ld[nm] = dma("sync", dst[:], src.ap())
        t_idb = dma("gpsimd", idb[:], identf.ap())
        ld["idf"] = dma("sync", idf[:], identf.ap()[0:64, 0:64])
        t_ac = op("scalar", lambda e: e.activation(a_col[:], a_col[:], AF.Exp),
                  waits=(ld["alog"],))
        t_ac = op("vector", lambda e: e.tensor_single_scalar(a_col[:], a_col[:],
                                                             -1.0, ALU.mult),
                  waits=(t_ac,))

        t_winA = [dma("gpsimd", winA[i], w_in.ap()[128 * i:128 * (i + 1), DI:2 * DI])
                  for i in range(8)]
        t_winB = [dma("gpsimd", winB[i], w_in.ap()[128 * i:128 * (i + 1), 0:DI])
                  for i in range(8)]
        t_wdt = [dma("gpsimd", wdt_sb[k], w_dt.ap()[128 * k:128 * (k + 1), :])
                 for k in range(16)]
        t_wx = [dma("gpsimd", wx_sb[:, 128 * k:128 * (k + 1)],
                    w_x.ap()[128 * k:128 * (k + 1), :]) for k in range(16)]
        t_wfcc = [dma("gpsimd", wfc_bf.ap()[1024 * i:1024 * (i + 1), :],
                      w_fc.ap()[1024 * i:1024 * (i + 1), :]) for i in range(4)]

        xa = [gh(20, 0), gh(20, 1), gh(21, 1)]
        xnbf2 = [G[21][:, 0:512].bitcast(BF16), G[21][:, 512:1024].bitcast(BF16)]

        last_x_use = [(), ()]
        prevt = [{}, {}]
        xnbf_free = [(), ()]
        tr_evac = [None, None]
        a_evac = {}
        muop = [None] * 17
        rop = [None] * 17
        a_tail = []
        sre_store2 = [None, None]
        c1_last_mm = None
        c1q = []
        ps_tok = [None] * 8
        s2op = [None] * 17

        def c1_group(c, g):
            nonlocal c1_last_mm
            evs = []
            for fo in range(4 * g, 4 * g + 4):
                b = fo % 4
                mm = None
                for k in range(8):
                    w0 = (t_was[k],)
                    if k == 0:
                        w0 += (ps_tok[b], a_evac[4 * c + 3])
                    mm = op("tensor",
                            lambda e, ps=PS[b],
                            w=winA[k][:, 128 * fo:128 * (fo + 1)],
                            r=xn2f[k][:, CH * c:CH * (c + 1)],
                            s=(k == 0), t=(k == 7):
                            e.matmul(ps[:, 0:512], w, r, start=s, stop=t),
                            waits=w0)
                c1_last_mm = mm
                t_ev = op("scalar", lambda e, ps=PS[b],
                          o=sre_b[4 * (g % 2) + fo % 4],
                          bi=bva[:, fo:fo + 1]:
                          e.activation(o, ps[:, 0:512], AF.Silu, bias=bi),
                          waits=(mm, sre_store2[g % 2], t_bva))
                ps_tok[b] = t_ev
                evs.append(t_ev)
            sre_store2[g % 2] = dma(
                "scalar",
                sc_sres.ap()[:, 4 * g:4 * g + 4, CH * c:CH * (c + 1)],
                G[22 + g % 2][:, 0:2048], waits=tuple(evs))

        def a_tail_emit(ti, p, t_xnn):
            nonlocal tr_evac
            t_tr = None
            for d8 in range(8):
                q = d8 % 2
                nrow = HALO if ti < 0 else 128
                pbv = PSB[q][0:128, 0:nrow]
                t_tr = op("tensor",
                          lambda e, i_=xnbf2[p][0:nrow, 128 * d8:128 * (d8 + 1)],
                          o=pbv, idn=idb[0:nrow, 0:nrow]:
                          e.transpose(o, i_, idn),
                          waits=(t_xnn, t_idb, tr_evac[q]))
                if ti < 0:
                    dst = xn2h[:, HALO * d8:HALO * (d8 + 1)]
                else:
                    dst = xn2f[d8][:, 128 * ti:128 * (ti + 1)]
                t_ev = op("vector", lambda e, i_=pbv, o=dst: e.tensor_copy(o, i_),
                          waits=(t_tr,))
                tr_evac[q] = t_ev
                a_evac[ti] = t_ev
            xnbf_free[p] = (t_tr,)
            if ti >= 14:
                a_tail.append(t_tr)

        pend_tail = None
        xuse = [(), (), ()]
        for ti in range(-1, 16):
            if pend_tail is not None:
                a_tail_emit(*pend_tail)
            p = ti % 2
            a3 = ti % 3
            nrow = HALO if ti < 0 else 128
            r0 = 0 if ti < 0 else HALO + 128 * ti
            pc = 16 if ti < 0 else ti
            xt = xa[a3][0:nrow, :]
            pv = prevt[p]
            if ti == 1:
                t_lnbbf = op("vector", lambda e: e.tensor_copy(lnb_bf[:], lnbc[:]),
                             waits=(ld["lnb"],))
                bvb_mm = None
                for fo in range(16):
                    for k in range(8):
                        bvb_mm = op("tensor", lambda e, w=winB[k][:, 128 * fo:128 * (fo + 1)],
                                    r=lnb_bf[:, k:k + 1], s=(k == 0), t=(k == 7), f=fo:
                                    e.matmul(PS[2][:, f:f + 1], w, r, start=s, stop=t),
                                    waits=(t_winB[k], t_lnbbf))
                t_bvb = op("vector", lambda e: e.tensor_copy(bvb[:], PS[2][:, 0:16]),
                           waits=(bvb_mm,))
                bva_mm = None
                for fo in range(16):
                    for k in range(8):
                        bva_mm = op("tensor", lambda e, w=winA[k][:, 128 * fo:128 * (fo + 1)],
                                    r=lnb_bf[:, k:k + 1], s=(k == 0), t=(k == 7), f=fo:
                                    e.matmul(PS[3][:, f:f + 1], w, r, start=s, stop=t),
                                    waits=(t_winA[k], t_lnbbf))
                t_bva = op("vector", lambda e: e.tensor_copy(bva[:], PS[3][:, 0:16]),
                           waits=(bva_mm,))
                t_wbs = [op("vector", lambda e, w=winB[k2], sc=lngc[:, k2:k2 + 1]:
                            e.tensor_scalar_mul(w, w, sc),
                            waits=(bvb_mm, ld["lng"], t_winB[k2])) for k2 in range(8)]
                t_was = [op("vector", lambda e, w=winA[k2], sc=lngc[:, k2:k2 + 1]:
                            e.tensor_scalar_mul(w, w, sc),
                            waits=(bva_mm, ld["lng"], t_winA[k2])) for k2 in range(8)]
                ps_tok[2] = t_bvb
                ps_tok[3] = t_bva

            def sc_(c, n=nrow, p_=p):
                return astat[0:n, 32 * p_ + c:32 * p_ + c + 1]

            def sg_(c0, c1, n=nrow, p_=p):
                return astat[0:n, 32 * p_ + c0:32 * p_ + c1]

            t_x = dma("sync", xt, x_in.ap()[r0:r0 + nrow, :], waits=xuse[a3])
            t_b1 = op("vector", lambda e, a=xt[:, 0:512], o=sg_(8, 14):
                      e.bn_stats(o, a), waits=(t_x, pv.get("ba")))
            t_b2 = op("vector", lambda e, a=xt[:, 512:1024], o=sg_(14, 20):
                      e.bn_stats(o, a), waits=(t_x, t_b1))
            t_ba = op("vector", lambda e, i_=sg_(8, 20), o=sg_(0, 2):
                      e.bn_aggr(o, i_),
                      waits=(t_b2, pv.get("muc"), pv.get("xnn"),
                             pv.get("l"), pv.get("d")))
            t_muc = op("vector", lambda e, i_=sc_(0), o=pstat[0:nrow, pc:pc + 1]:
                       e.tensor_copy(o, i_), waits=(t_ba,))
            muop[pc] = t_ba
            t_l = op("scalar", lambda e, a=sc_(1), o=sc_(2), bi=c_eps[0:nrow, :]:
                     e.activation(o, a, AF.Ln, bias=bi),
                     waits=(t_ba, t_ct, pv.get("r")))
            t_r = op("scalar", lambda e, a=sc_(2),
                     o=pstat[0:nrow, 17 + pc:18 + pc], sc=c_mh[0:nrow, :]:
                     e.activation(o, a, AF.Exp, scale=sc), waits=(t_l,))
            rop[pc] = t_r
            t_rr = op("vector", lambda e, r=pstat[0:nrow, 17 + pc:18 + pc],
                      o=sc_(3):
                      e.tensor_tensor(o, r, r, ALU.mult),
                      waits=(t_r, pv.get("d")))
            t_d = op("vector", lambda e, a=sc_(1), b=sc_(3), o=sc_(4):
                     e.tensor_tensor(o, a, b, ALU.mult),
                     waits=(t_rr, pv.get("l2"), pv.get("s2")))
            t_s2 = op("vector", lambda e, a=sc_(4),
                      o=pstat[0:nrow, 34 + pc:35 + pc]:
                      e.tensor_single_scalar(o, a, EPS, ALU.add), waits=(t_d,))
            s2op[pc] = t_s2
            t_l2 = op("scalar", lambda e, a=sc_(4), o=sc_(5), bi=c_eps[0:nrow, :]:
                      e.activation(o, a, AF.Ln, bias=bi),
                      waits=(t_d, pv.get("r2")))
            t_r2 = op("scalar", lambda e, a=sc_(5), o=sc_(6), sc=c_mh[0:nrow, :]:
                      e.activation(o, a, AF.Exp, scale=sc),
                      waits=(t_l2, pv.get("rc")))
            t_rc = op("vector", lambda e, a=sc_(6),
                      r=pstat[0:nrow, 17 + pc:18 + pc], o=sc_(7):
                      e.tensor_tensor(o, a, r, ALU.mult),
                      waits=(t_r2, pv.get("xnn")))
            t_xnn = op("vector", lambda e, a=xt, o=xnbf2[p][0:nrow, 0:1024],
                       m=sc_(0), r=sc_(7):
                       e.tensor_scalar(o, a, m, r, ALU.subtract, ALU.mult),
                       waits=(t_rc,) + xnbf_free[p], hard=(t_ba,))
            xuse[a3] = (t_xnn,)
            prevt[p] = dict(ba=t_ba, muc=t_muc, l=t_l, r=t_r, d=t_d, s2=t_s2,
                            l2=t_l2, r2=t_r2, rc=t_rc, xnn=t_xnn)
            if ti >= 14:
                a_tail += [t_xnn]
            pend_tail = (ti, p, t_xnn)
            if ti >= 0 and ti % 4 == 3:
                for g in range(4):
                    c1q.append((ti // 4, g))
            if ti >= 4 and c1q:
                c1_group(*c1q.pop(0))
        a_tail_emit(*pend_tail)
        while c1q:
            c1_group(*c1q.pop(0))

        t_wprc = [dma("gpsimd", wpr_bf.ap()[256 * i:256 * (i + 1), :],
                      w_pr.ap()[256 * i:256 * (i + 1), :],
                      waits=(a_evac[15],) if i == 0 else ())
                  for i in range(4)]
        ps_tok[6] = tr_evac[0]
        ps_tok[7] = tr_evac[1]
        a_guard = tuple(t for t in a_tail + [tr_evac[0], tr_evac[1],
                                             sre_store2[0], sre_store2[1]]
                        if t is not None)
        sre_tok = [sre_store2[0], sre_store2[1]]

        t_wfcT = [None] * 8

        t_xc = [None] * 16
        xc_readers = [None] * 16
        hist_tok = None
        drow_tok = None
        t_dl_dmas = []
        c2_last_mm = None
        c2_last_wdt = None
        u_tok = [None, None]
        sp_tok = [None, None]
        acc_tok = [None, None]
        xcf_tok = [None, None]
        xp_tok = [None, None]

        for c in range(NCH):
            for fo in range(16):
                b = fo % 2
                bk = fo % 3
                if c == 0:
                    mmh = None
                    for k in range(8):
                        w0 = (t_wbs[k], ps_tok[3], a_evac[-1]) if k == 0 \
                            else (t_wbs[k],)
                        mmh = op("tensor",
                                 lambda e, ps=PS[3],
                                 w=winB[k][:, 128 * fo:128 * (fo + 1)],
                                 r=xn2h[:, HALO * k:HALO * (k + 1)],
                                 s=(k == 0), t=(k == 7):
                                 e.matmul(ps[:, 0:HALO], w, r, start=s, stop=t),
                                 waits=w0)
                    t_h = op("vector",
                             lambda e, ps=PS[3], o=hist[:, 3 * fo:3 * fo + 3],
                             bi=bvb[:, fo:fo + 1]:
                             e.tensor_scalar(o, ps[:, 0:HALO], bi, msk[:, 0:1],
                                             ALU.add, ALU.mult),
                             waits=(mmh, ld["msk"], hist_tok, t_bvb))
                    ps_tok[3] = t_h
                    hist_tok = t_h
                mm = None
                for k in range(8):
                    w0 = (t_wbs[k], ps_tok[bk]) if k == 0 else (t_wbs[k],)
                    mm = op("tensor",
                            lambda e, ps=PS[bk], w=winB[k][:, 128 * fo:128 * (fo + 1)],
                            r=xn2f[k][:, CH * c:CH * (c + 1)], s=(k == 0), t=(k == 7):
                            e.matmul(ps[:, 0:512], w, r, start=s, stop=t), waits=w0)
                c2_last_mm = mm
                t_ev = op("scalar", lambda e, ps=PS[bk], o=xp_b[b][:, 3:515],
                          bi=bvb[:, fo:fo + 1]:
                          e.activation(o, ps[:, 0:512], AF.Identity, bias=bi),
                          waits=(mm, xp_tok[b], t_bvb) +
                          (a_guard if c == 0 and fo < 2 else ()))
                ps_tok[bk] = t_ev
                t_hi = op("vector",
                          lambda e, o=xp_b[b][:, 0:3], i_=hist[:, 3 * fo:3 * fo + 3]:
                          e.tensor_copy(o, i_), waits=(hist_tok, t_ev))
                t_a0 = op("vector", lambda e, o=acc_b[b], i_=xp_b[b][:, 0:512],
                          s_=cw[:, 4 * fo:4 * fo + 1], bi=cb[:, fo:fo + 1]:
                          e.tensor_scalar(o, i_, s_, bi, ALU.mult, ALU.add),
                          waits=(t_ev, t_hi, ld["cw"], ld["cb"], acc_tok[b],
                                 sp_tok[0], sp_tok[1]))
                t_a1 = op("vector", lambda e, o=acc_b[b], i_=xp_b[b][:, 1:513],
                          s_=cw[:, 4 * fo + 1:4 * fo + 2], a2=acc_b[b]:
                          e.scalar_tensor_tensor(o, i_, s_, a2, ALU.mult, ALU.add),
                          waits=(t_a0,))
                t_a2 = op("vector", lambda e, o=acc_b[b], i_=xp_b[b][:, 2:514],
                          s_=cw[:, 4 * fo + 2:4 * fo + 3], a2=acc_b[b]:
                          e.scalar_tensor_tensor(o, i_, s_, a2, ALU.mult, ALU.add),
                          waits=(t_a1,))
                t_a3 = op("vector", lambda e, o=acc_b[b], i_=xp_b[b][:, 3:515],
                          s_=cw[:, 4 * fo + 3:4 * fo + 4], a2=acc_b[b]:
                          e.scalar_tensor_tensor(o, i_, s_, a2, ALU.mult, ALU.add),
                          waits=(t_a2,))
                t_hs = op("vector", lambda e, o=hist[:, 3 * fo:3 * fo + 3],
                          i_=xp_b[b][:, 512:515]:
                          e.tensor_copy(o, i_), waits=(t_ev, t_hi))
                hist_tok = t_hs
                xp_tok[b] = t_a3
                t_si = op("scalar", lambda e, o=xcf_b[b], i_=acc_b[b]:
                          e.activation(o, i_, AF.Silu),
                          waits=(t_a3, xcf_tok[b]))
                acc_tok[b] = t_si
                t_uo = op("vector", lambda e, o=u_b[b], i_=xcf_b[b], s_=dv[:, fo:fo + 1]:
                          e.tensor_scalar_mul(o, i_, s_),
                          waits=(t_si, ld["dv"], u_tok[b]))
                t_ud = dma("sync", sc_u.ap()[:, fo, CH * c:CH * (c + 1)], u_b[b],
                           waits=(t_uo,))
                u_tok[b] = t_ud
                acc_tok[b] = t_ud
                t_xbf = op("vector", lambda e, o=xcbf[fo], i_=xcf_b[b]:
                           e.tensor_copy(o, i_), waits=(t_si, xc_readers[fo]) +
                           (a_guard if c == 0 else ()))
                xcf_tok[b] = t_xbf
                t_xc[fo] = t_xbf
            mmb = None
            for k in range(16):
                w0 = (t_xc[k], t_wx[k], ps_tok[4]) if k == 0 else (t_xc[k], t_wx[k])
                mmb = op("tensor",
                         lambda e, ps=PS[4], w=wx_sb[:, 128 * k:128 * k + 64],
                         r=xcbf[k], s=(k == 0), t=(k == 15):
                         e.matmul(ps[0:64, 0:512], w, r, start=s, stop=t), waits=w0)
            mmc = None
            for k in range(16):
                w0 = (t_xc[k], t_wx[k], ps_tok[3]) if k == 0 else (t_xc[k], t_wx[k])
                mmc = op("tensor",
                         lambda e, ps=PS[3], w=wx_sb[:, 128 * k + 64:128 * (k + 1)],
                         r=xcbf[k], s=(k == 0), t=(k == 15):
                         e.matmul(ps[0:64, 0:512], w, r, start=s, stop=t), waits=w0)
            t_bc = op("vector", lambda e, ps=PS[4], o=bst_bf[:, CH * c:CH * (c + 1)]:
                      e.tensor_copy(o, ps[0:64, 0:512]), waits=(mmb, ps_tok[4]) +
                      (a_guard if c == 0 else ()))
            ps_tok[4] = t_bc
            t_cc2 = op("vector", lambda e, ps=PS[3], o=cst_bf[:, CH * c:CH * (c + 1)]:
                       e.tensor_copy(o, ps[0:64, 0:512]), waits=(mmc,))
            ps_tok[3] = t_cc2
            for k in range(16):
                xc_readers[k] = mmc
            sp_mm = None
            lns = [None] * 16
            spmm_tok = [None, None]

            def emit_spmm(co2):
                nonlocal sp_mm
                w0 = (lns[co2], t_onec) + ((ps_tok[3],) if co2 == 0 else ())
                sp_mm = op("tensor", lambda e, g=sp_g2[co2 % 2],
                           s=(co2 == 0), t=(co2 == 15):
                           e.matmul(PS[3][0:1, 0:512], ones_c[:], g,
                                    start=s, stop=t), waits=w0)
                spmm_tok[co2 % 2] = sp_mm

            for co in range(16):
                b = 5 + co % 3
                if co >= 2:
                    emit_spmm(co - 2)
                mm = None
                for k in range(16):
                    w0 = ((t_xc[k], t_wdt[k], ps_tok[b]) if k == 0
                          else (t_xc[k], t_wdt[k]))
                    mm = op("tensor",
                            lambda e, ps=PS[b],
                            l=wdt_sb[k][:, 128 * co:128 * (co + 1)],
                            r=xcbf[k], s=(k == 0), t=(k == 15):
                            e.matmul(ps[:, 0:512], l, r, start=s, stop=t),
                            waits=w0)
                c2_last_wdt = mm
                for k in range(16):
                    xc_readers[k] = mm
                t_exo = op("scalar", lambda e, ps=PS[b], bi=bdtc[:, co:co + 1]:
                           e.activation(sp_f, ps[:, 0:512], AF.Exp, bias=bi),
                           waits=(mm, ld["bdt"], sp_tok[0]) +
                           ((u_tok[0], u_tok[1]) if co == 0 else ()))
                ps_tok[b] = t_exo
                t_ln = op("scalar", lambda e, g=sp_g2[co % 2]:
                          e.activation(g, sp_f, AF.Ln, bias=c_one),
                          waits=(t_exo, spmm_tok[co % 2]))
                sp_tok[0] = t_ln
                lns[co] = t_ln
                sp_tok[1] = t_ln
            emit_spmm(14)
            emit_spmm(15)
            sp_tok[1] = sp_mm
            t_de = op("vector", lambda e:
                      e.tensor_single_scalar(drow, PS[3][0:1, 0:512],
                                             1.0 / DI, ALU.mult),
                      waits=(sp_mm, drow_tok) + (a_guard if c == 0 else ()))
            ps_tok[3] = t_de
            t_dd = dma("scalar", sc_dl.ap()[0:1, CH * c:CH * (c + 1)], drow,
                       waits=(t_de,))
            drow_tok = t_dd
            t_dl_dmas.append(t_dd)
            for j in range(4):
                t_wfcT[c] = dma("scalar", wfcT[c][:, 1024 * j:1024 * (j + 1)],
                                wfc_bf.ap()[1024 * j:1024 * (j + 1),
                                            128 * c:128 * (c + 1)],
                                waits=(c1_last_mm, t_wfcc[j]), transpose=True)

        t_ad = [None] * 4
        t_pe = [None] * 4
        prev_dl = ()
        for j in range(4):
            t_dl = dma("gpsimd", dtmp[:, :],
                       sc_dl.ap()[0:1, CH * j:CH * (j + 1)].partition_broadcast(DS),
                       waits=(t_dl_dmas[j], xp_tok[0], xp_tok[1],
                              sre_tok[0], sre_tok[1]) + prev_dl)
            t_ad[j] = op("scalar",
                         lambda e, o=ad_sb[:, CH * j:CH * (j + 1)]:
                         e.activation(o, dtmp, AF.Exp, scale=a_col[:]),
                         waits=(t_dl, t_ac, sp_tok[0], sp_tok[1],
                                acc_tok[0], acc_tok[1], xcf_tok[0], xcf_tok[1],
                                u_tok[0], u_tok[1]))
            t_pe[j] = op("vector", lambda e, o=pend[:, j:j + 1]:
                         e.reduce_sum(o, dtmp, axis=AX), waits=(t_dl,))
            prev_dl = (t_ad[j], t_pe[j])
        t_pt = op("vector", lambda e: e.reduce_sum(pend[:, 4:5], pend[:, 0:4], axis=AX),
                  waits=tuple(t_pe))
        t_dbgad = None
        if debug:
            t_dbgad = dma("gpsimd", dbg["ad"].ap(), ad_sb, waits=tuple(t_ad))
        t_sc1 = op("vector", lambda e: e.tensor_tensor_scan(
            hsl, ad_sb, bst_bf, 0.0, ALU.mult, ALU.add),
            waits=tuple(t_ad) + (ps_tok[4], u_tok[0], u_tok[1],
                                 sp_tok[0], sp_tok[1], xp_tok[0], xp_tok[1],
                                 sre_tok[0], sre_tok[1], drow_tok))
        t_hg0 = op("vector", lambda e: e.tensor_copy(hg[:, 0:1], hsl[:, TPC - 1:TPC]),
                   waits=(t_sc1,))
        t_dbgh1 = None
        if debug:
            t_dbgh1 = dma("gpsimd", dbg["hs1"].ap(), hsl, waits=(t_sc1,))
        t_hg1 = op("scalar", lambda e: e.activation(hg[:, 1:2], a_col[:], AF.Exp,
                                                    scale=pend[:, 4:5]),
                   waits=(t_pt, t_ac))
        t_hgt = op("tensor", lambda e: e.transpose(PS[2][0:2, 0:64], hg[:], idf[:]),
                   waits=(t_hg0, t_hg1, ps_tok[2], ld["idf"]))
        t_hge = op("vector", lambda e: e.tensor_copy(hgT[:], PS[2][0:2, 0:64]),
                   waits=(t_hgt,))
        ps_tok[2] = t_hge
        t_ci0 = dma("gpsimd", cc_in.ap()[0:1, 0:DS], hgT[0:1, :], waits=(t_hge,))
        t_ci1 = dma("gpsimd", cc_in.ap()[0:1, DS:2 * DS], hgT[1:2, :], waits=(t_hge,))
        t_cc = S.add("gpsimd", lambda e: e.collective_compute(
            "AllGather", ALU.bypass, replica_groups=[list(range(8))],
            ins=[cc_in.ap().opt()], outs=[cc_out.ap().opt()]),
            waits=(t_ci0, t_ci1), dma=False)
        t_ag = dma("gpsimd", ag[:], cc_out.ap(), waits=(t_cc,))
        t_agt = op("tensor", lambda e: e.transpose(PS[2][0:64, 0:8], ag[:, 0:DS],
                                                   idf[0:8, 0:8]),
                   waits=(t_ag, ps_tok[2], ld["idf"]))
        t_agh = op("vector", lambda e: e.tensor_copy(agH[:], PS[2][0:64, 0:8]),
                   waits=(t_agt,))
        t_agt2 = op("tensor", lambda e: e.transpose(PS[2][0:64, 8:16],
                                                    ag[:, DS:2 * DS], idf[0:8, 0:8]),
                    waits=(t_ag, t_agh))
        t_age = op("vector", lambda e: e.tensor_copy(agG[:], PS[2][0:64, 8:16]),
                   waits=(t_agt2,))
        ps_tok[2] = t_age
        t_prev = op("vector", lambda e: e.memset(dtile[:, 0:1], 0.0),
                    waits=(t_age, t_agh))
        for j in range(1, 8):
            if j == 4:
                t_prev = op("vector", lambda e: e.memset(dtile[:, 4:5], 0.0),
                            waits=(t_prev,))
                continue
            t_m = op("vector", lambda e, j_=j: e.tensor_tensor(
                dcur[:, 0:1], dtile[:, j_ - 1:j_], agG[:, j_ - 1:j_], ALU.mult),
                waits=(t_prev,))
            t_prev = op("vector", lambda e, j_=j: e.tensor_tensor(
                dtile[:, j_:j_ + 1], dcur[:, 0:1], agH[:, j_ - 1:j_], ALU.add),
                waits=(t_m,))
        t_dsl = op("vector", lambda e: e.tensor_tensor(dsel[:], dtile[:], sel_sb[:],
                                                       ALU.mult),
                   waits=(t_prev, ld["sel"]))
        t_hin = op("vector", lambda e: e.reduce_sum(hin[:, 0:1], dsel[:], axis=AX),
                   waits=(t_dsl,))
        t_sc2 = op("vector", lambda e: e.tensor_tensor_scan(
            ad_sb, ad_sb, bst_bf, hin[:, 0:1], ALU.mult, ALU.add),
            waits=(t_hg0, t_dbgad), hard=(t_hin,))
        t_dbgh2 = None
        if debug:
            t_dbgh2 = dma("gpsimd", dbg["hs2"].ap(), ad_sb, waits=(t_sc2,))
        t_cy = op("vector", lambda e: e.tensor_tensor(cy_bf, ad_sb, cst_bf,
                                                      ALU.mult),
                  waits=(t_sc2, t_dbgh2))
        t_dbgbc = None
        if debug:
            t_dbgbc = dma("gpsimd", dbg["bc"].ap(), G[22][:], waits=(t_cy,))
            t_dbgcy = dma("gpsimd", dbg["cy"].ap(), G[24][:, 0:1024], waits=(t_cy,))
        t_ysd = [None] * 4
        t_ysmm = None
        for j in range(4):
            t_ysmm = op("tensor", lambda e, j_=j: e.matmul(
                PS[3][0:1, 0:512], ones_c[0:64, :], cy_bf[:, 512 * j_:512 * (j_ + 1)],
                start=True, stop=True),
                waits=(t_cy, t_onec, ps_tok[3]))
            t_e = op("vector", lambda e: e.tensor_copy(dtmp[0:1, :],
                                                       PS[3][0:1, 0:512]),
                     waits=(t_ysmm, t_ysd[j - 1] if j else None)
                     + (prev_dl + (t_pe[3],) if j == 0 else ()))
            ps_tok[3] = t_e
            t_ysd[j] = dma("gpsimd", sc_ys.ap()[0:1, 512 * j:512 * (j + 1)],
                           dtmp[0:1, :], waits=(t_e,))

        t_woc = [dma("gpsimd", wout_d.ap()[1024 * i:1024 * (i + 1), :],
                     w_out.ap()[1024 * i:1024 * (i + 1), :]) for i in range(2)]

        for i in range(4, 8):
            t_wfcT[i] = dma("scalar", wfcT[i],
                            wfc_bf.ap()[:, 128 * i:128 * (i + 1)],
                            waits=(c2_last_mm,) + tuple(t_wfcc), transpose=True)
        t_wprT = [dma("scalar", wprT[j],
                      wpr_bf.ap()[:, 128 * j:128 * (j + 1)],
                      waits=(c2_last_wdt,) + tuple(t_wprc), transpose=True)
                  for j in range(32)]

        h_tok = [None] * 32
        h_readers = [None] * 32
        e_tok = [None, None]
        rl_tok = [None, None]
        cnt_tok2 = [None, None]
        e_last_mm1 = None
        e_last_mm2 = None
        t_ysmm_last = t_ysmm
        for ec in range(NECH):
            for f in range(32):
                b = f % 2
                mm = None
                for k in range(8):
                    w0 = ((t_wfcT[k], ps_tok[b]) if k == 0
                          else (t_wfcT[k],))
                    mm = op("tensor",
                            lambda e, ps=PS[b], w=wfcT[k][:, 128 * f:128 * (f + 1)],
                            r=xnbf[k][:, ECH * ec:ECH * (ec + 1)],
                            s=(k == 0), t=(k == 7):
                            e.matmul(ps[:, 0:512], w, r, start=s, stop=t), waits=w0)
                e_last_mm1 = mm
                rl = rl_b[b]
                wrl = (mm, rl_tok[b])
                if ec == 0 and f < 2:
                    wrl += (t_ysmm_last, t_ad[b], acc_tok[b], xcf_tok[b])
                t_rl = op("vector", lambda e, ps=PS[b], o=rl:
                          e.tensor_single_scalar(o, ps[:, 0:512], 0.0, ALU.max),
                          waits=wrl)
                ps_tok[b] = t_rl
                wsq = (t_rl, h_readers[f])
                if ec == 0:
                    if f < 16:
                        wsq += (xc_readers[f % 16],)
                    elif f < 24:
                        wsq += (t_cy,)
                    else:
                        wsq += (tuple(t_ysd) + (t_hg0, drow_tok,
                                                xp_tok[0], xp_tok[1],
                                                sre_tok[0], sre_tok[1],
                                                t_ad[3],))
                t_sq = op("vector", lambda e, i_=rl, o=htile[f]:
                          e.tensor_tensor(o, i_, i_, ALU.mult), waits=wsq)
                rl_tok[b] = t_sq
                h_tok[f] = t_sq
            for t4 in range(4):
                tg = 4 * ec + t4
                mmd = [None, None]
                for k in range(32):
                    for dn in range(2):
                        b = 2 + dn
                        w0 = (h_tok[k], t_wprT[k])
                        if k == 0:
                            w0 += (ps_tok[b],)
                        mmd[dn] = op("tensor",
                                     lambda e, ps=PS[b],
                                     l=htile[k][:, 128 * t4:128 * (t4 + 1)],
                                     r=wprT[k][:, 512 * dn:512 * (dn + 1)],
                                     s=(k == 0), t=(k == 31):
                                     e.matmul(ps[:, 0:512], l, r, start=s, stop=t),
                                     waits=w0)
                e_last_mm2 = mmd[1]
                for k in range(32):
                    h_readers[k] = mmd[1]
                for dn in range(2):
                    b = 2 + dn
                    cn = cnt_b[dn]
                    w0 = (mmd[dn], ps_tok[6 + dn])
                    if ec == 0 and t4 == 0:
                        w0 += (t_ysmm_last, t_ad[2 + dn])
                    t_z = op("vector", lambda e, ps=PS[b], o=PS[6 + dn][:, 0:512],
                             sc=pstat[:, 34 + tg:35 + tg]:
                             e.tensor_scalar_mul(o, ps[:, 0:512], sc),
                             waits=w0, hard=(s2op[tg],))
                    ps_tok[b] = t_z
                    t_c0 = op("vector", lambda e, z=PS[6 + dn], o=cn:
                              e.tensor_single_scalar(o, z[:, 0:512], THS[0],
                                                     ALU.is_gt),
                              waits=(t_z, cnt_tok2[dn]))
                    t_c = t_c0
                    for th in THS[1:4]:
                        t_c = op("vector", lambda e, z=PS[6 + dn], o=cn, a2=cn,
                                 t_=th:
                                 e.scalar_tensor_tensor(o, z[:, 0:512], t_, a2,
                                                        ALU.is_gt, ALU.add),
                                 waits=(t_c,))
                    t_c4 = op("vector",
                              lambda e, z=PS[6 + dn],
                              o=cn.bitcast(BF16)[:, 0:512], a2=cn:
                              e.scalar_tensor_tensor(o, z[:, 0:512], THS[4], a2,
                                                     ALU.is_gt, ALU.add),
                              waits=(t_c,))
                    ps_tok[6 + dn] = t_c4
                    t_cd = dma("sync" if dn == 0 else "gpsimd",
                               sc_cnt.ap()[128 * tg:128 * (tg + 1),
                                           512 * dn:512 * (dn + 1)],
                               cn.bitcast(BF16)[:, 0:512], waits=(t_c4,))
                    cnt_tok2[dn] = t_cd
                    e_tok[dn] = t_cd

        t_wo = [dma("scalar", wout_sb[k],
                    wout_d.ap()[128 * k:128 * (k + 1), :],
                    waits=(e_last_mm1,) + tuple(t_woc)) for k in range(16)]

        t_ysb = dma("gpsimd", ysb[:], sc_ys.ap().partition_broadcast(128),
                    waits=tuple(t_ysd) + (e_last_mm1,))
        y2_tok_all = [None] * NCH
        y2_readers = [[None] * 16, [None] * 16]
        fu_tok = [(), ()]
        fsr_tok = [(), ()]
        ft9_tok = [None, None]
        fxn_tok = [None, None]
        fx_tok = [(), (), (), ()]
        cnt8 = [G[22][:].bitcast(BF16)[:, i * 1024:(i + 1) * 1024]
                for i in range(4)] + \
               [G[23][:].bitcast(BF16)[:, i * 1024:(i + 1) * 1024]
                for i in range(4)]
        cnt_readers = [None] * 8
        s_tok4 = [None] * 4
        s_tok2 = [None, None]
        g_tok2 = [None, None]
        sg_tok2 = [None, None]
        o_tok2 = [None, None]
        out_dmas = []

        def f_y2stage(fc):
            pb = fc % 2
            y2t = y2t2[pb]
            y2_tok = [None] * 16
            for bq in range(4):
                first = fc == 0 and bq < 2
                wu = (u_tok[0], u_tok[1], e_last_mm1) if first else fu_tok[bq % 2]
                ws = (sre_tok[0], e_last_mm1) if first else fsr_tok[bq % 2]
                t_lu = dma("sync", fub[bq % 2][:],
                           sc_u.ap()[:, 4 * bq:4 * bq + 4, CH * fc:CH * (fc + 1)],
                           waits=wu)
                t_ls = dma("scalar", fsrb[bq % 2][:],
                           sc_sres.ap()[:, 4 * bq:4 * bq + 4, CH * fc:CH * (fc + 1)],
                           waits=ws)
                t9s, y2s = [], []
                for j in range(4):
                    fo = 4 * bq + j
                    b = fo % 2
                    t_t9 = op("gpsimd", lambda e, o=f_t9[b],
                              i_=fub[bq % 2][:, 512 * j:512 * (j + 1)],
                              y=ysb[:, CH * fc:CH * (fc + 1)]:
                              e.tensor_tensor(o, i_, y, ALU.add),
                              waits=(t_lu, t_ysb, ft9_tok[b])
                              + ((e_tok[b], rl_tok[b], t_ysmm_last, t_ad[b],
                                  acc_tok[b], xcf_tok[b])
                                 if fc == 0 and fo < 2 else ()))
                    t_y2 = op("vector", lambda e, o=y2t[fo], a=f_t9[b],
                              s_=fsrb[bq % 2][:, 512 * j:512 * (j + 1)]:
                              e.tensor_tensor(o, a, s_, ALU.mult),
                              waits=(t_t9, t_ls, y2_readers[pb][fo])
                              + ((e_last_mm1,) if fc == 0 else ())
                              + ((e_last_mm2,) if fc == 1 else ()))
                    ft9_tok[b] = t_y2
                    y2_tok[fo] = t_y2
                    t9s.append(t_t9)
                    y2s.append(t_y2)
                fu_tok[bq % 2] = tuple(t9s)
                fsr_tok[bq % 2] = tuple(y2s)
            y2_tok_all[fc] = y2_tok

        t4_state = [None] * NCH

        def f_t4A(fc):
            pb = fc % 2
            y2t = y2t2[pb]
            y2_tok = y2_tok_all[fc]
            st = []
            for t4 in range(4):
                p = t4 % 2
                tg = 4 * fc + t4
                ci = t4 + 4 * (fc % 2)
                t_lx = dma("sync", f_x4[t4],
                           x_in.ap()[HALO + 128 * tg:HALO + 128 * (tg + 1), :],
                           waits=fx_tok[t4] if fx_tok[t4] else (e_last_mm2,))
                t_rec = op("vector", lambda e, o=f_xn[p], i_=f_x4[t4],
                           m=pstat[:, tg:tg + 1], r=pstat[:, 17 + tg:18 + tg]:
                           e.tensor_scalar(o, i_, m, r, ALU.subtract, ALU.mult),
                           waits=(t_lx,)
                           + ((fxn_tok[p],) if fxn_tok[p] else (e_last_mm2,)),
                           hard=(muop[tg], rop[tg]))
                t_lc = dma("scalar", cnt8[ci],
                           sc_cnt.ap()[128 * tg:128 * (tg + 1), :],
                           waits=(cnt_readers[ci], cnt_tok2[0], cnt_tok2[1])
                           + ((e_last_mm2,) if fc < 2 else ()))
                s_half = [None, None]
                mmw = [None, None]
                for k in range(16):
                    for dn in range(2):
                        b = 4 + 2 * (t4 % 2) + dn
                        w0 = (y2_tok[k], t_wo[k])
                        if k == 0:
                            w0 += (ps_tok[b],)
                        mmw[dn] = op("tensor",
                                     lambda e, ps=PS[b],
                                     l=y2t[k][:, 128 * t4:128 * (t4 + 1)],
                                     r=wout_sb[k][:, 512 * dn:512 * (dn + 1)],
                                     s=(k == 0), t=(k == 15):
                                     e.matmul(ps[:, 0:512], l, r, start=s, stop=t),
                                     waits=w0)
                for k in range(16):
                    y2_readers[pb][k] = mmw[1]
                for dn in range(2):
                    b = 4 + 2 * (t4 % 2) + dn
                    t_s = op("vector",
                             lambda e, ps=PS[b],
                             o=f_s4[t4][:, 512 * dn:512 * (dn + 1)],
                             x_=f_xn[p][:, 512 * dn:512 * (dn + 1)]:
                             e.tensor_tensor(o, ps[:, 0:512], x_, ALU.add),
                             waits=(mmw[dn], t_rec)
                             + ((s_tok4[t4],) if s_tok4[t4] else (e_last_mm2,)))
                    ps_tok[b] = t_s
                    s_half[dn] = t_s
                fxn_tok[p] = s_half[1]
                st.append((t4, p, tg, ci, t_lx, t_lc, t_rec, s_half))
            t4_state[fc] = st

        def f_t4B(fc):
            for (t4, p, tg, ci, t_lx, t_lc, t_rec, s_half) in t4_state[fc]:
                t_g = op("scalar", lambda e, i_=cnt8[ci], o=f_g2[p]:
                         e.activation(o, i_, AF.Silu, scale=c_p2),
                         waits=(t_lc,)
                         + ((g_tok2[p],) if g_tok2[p] else (e_last_mm2,)))
                cnt_readers[ci] = t_g
                t_sg = op("vector", lambda e, o=f_sg2[p], a=f_s4[t4], g=f_g2[p]:
                          e.tensor_tensor(o, a, g, ALU.mult),
                          waits=(t_g, s_half[0], s_half[1])
                          + ((sg_tok2[p],) if sg_tok2[p] else (e_last_mm2,)))
                s_tok4[t4] = t_sg
                g_tok2[p] = t_sg
                t_o = op("vector", lambda e, o=f_o2[p], a=f_sg2[p], x_=f_x4[t4]:
                         e.tensor_tensor(o, a, x_, ALU.add),
                         waits=(t_sg, t_lx)
                         + ((o_tok2[p],) if o_tok2[p] else (e_last_mm2,)))
                sg_tok2[p] = t_o
                t_od = dma("sync", out_t.ap()[128 * tg:128 * (tg + 1), :],
                           f_o2[p], waits=(t_o,))
                o_tok2[p] = t_od
                fx_tok[t4] = (t_o, t_rec)
                out_dmas.append(t_od)

        for stage in range(NCH + 1):
            if stage >= 1:
                f_t4A(stage - 1)
            if stage < NCH:
                f_y2stage(stage)
            if stage >= 1:
                f_t4B(stage - 1)

        if debug:
            tl = None
            for nm, src in (("u", sc_u), ("sres", sc_sres),
                            ("cnt_", sc_cnt), ("ys", sc_ys)):
                tl = dma("gpsimd", dbg[nm].ap(), src.ap(),
                         waits=tuple(out_dmas[-2:]) + ((tl,) if tl else ()))
            dma("gpsimd", dbg["delta"].ap(), sc_dl.ap(), waits=(out_dmas[-1],))
            dma("gpsimd", dbg["hgt"].ap(), hgT[:], waits=(t_hge, out_dmas[-1]))

        op("gpsimd", lambda e: e.engine_nop(), waits=tuple(out_dmas))

        S.finalize()
        with nc.Block() as block:
            block.sync(S.emitter("sync", sems))
            block.scalar(S.emitter("scalar", sems))
            block.vector(S.emitter("vector", sems))
            block.tensor(S.emitter("tensor", sems))
            block.gpsimd(S.emitter("gpsimd", sems))

    return nc


_NC_CACHE = {}


def _get_nc(debug=False):
    if debug not in _NC_CACHE:
        _NC_CACHE[debug] = _build(debug)
    return _NC_CACHE[debug]


def _prep_inputs(inputs):
    x = np.ascontiguousarray(inputs["L0_attn_x"], dtype=np.float32)
    B, T, _ = x.shape
    cps = 8 // B
    L = T // cps
    conv_w = np.ascontiguousarray(
        np.asarray(inputs["L0_attn_conv_w"], np.float32)
        .reshape(16, 128, 4).transpose(1, 0, 2).reshape(128, 64))
    conv_b = np.ascontiguousarray(
        np.asarray(inputs["L0_attn_conv_b"], np.float32).reshape(16, 128).T)
    d_r = np.ascontiguousarray(
        np.asarray(inputs["L0_attn_D"], np.float32).reshape(16, 128).T)
    common = dict(
        w_in=np.ascontiguousarray(inputs["L0_attn_W_in"], np.float32),
        w_dt=np.ascontiguousarray(inputs["L0_attn_W_dt"], np.float32),
        w_x=np.ascontiguousarray(inputs["L0_attn_W_x"], np.float32),
        w_out=np.ascontiguousarray(inputs["L0_attn_W_out"], np.float32),
        w_fc=np.ascontiguousarray(inputs["L0_ffn_W_fc"], np.float32),
        w_pr=np.ascontiguousarray(inputs["L0_ffn_W_proj"], np.float32),
        lngT=np.ascontiguousarray(
            np.asarray(inputs["L0_attn_ln_g"], np.float32).reshape(8, 128).T),
        lnbT=np.ascontiguousarray(
            np.asarray(inputs["L0_attn_ln_b"], np.float32).reshape(8, 128).T),
        convw=conv_w, convb=conv_b, dvec=d_r,
        bdt=np.ascontiguousarray(
            np.asarray(inputs["L0_attn_b_dt"], np.float32).reshape(16, 128).T),
        alog=np.ascontiguousarray(
            np.asarray(inputs["L0_attn_A_log"], np.float32).reshape(DS, 1)),
        identf=np.eye(128, dtype=np.float32),
    )
    in_maps = []
    for core in range(8):
        b, c = core // cps, core % cps
        t0 = c * L
        xs = np.zeros((HALO + L, D), np.float32)
        xs[HALO:] = x[b, t0:t0 + L]
        if c > 0:
            xs[:HALO] = x[b, t0 - HALO:t0]
        sel = np.zeros((DS, 8), np.float32)
        sel[:, core] = 1.0
        mask = np.full((128, 1), 0.0 if c == 0 else 1.0, np.float32)
        m = dict(common)
        m.update(x_in=np.ascontiguousarray(xs), selv=sel, maskv=mask)
        in_maps.append(m)
    return in_maps, (B, T, cps, L)


def kernel(**inputs):
    in_maps, (B, T, cps, L) = _prep_inputs(inputs)
    nc = _get_nc(False)
    res = run_bass_kernel_spmd(nc, in_maps, core_ids=list(range(8)))
    out = np.empty((B, T, D), np.float32)
    for core in range(8):
        b, c = core // cps, core % cps
        out[b, c * L:(c + 1) * L] = res.results[core]["out"]
    return out


# version 31
# speedup vs baseline: 1.0153x; 1.0153x over previous
import numpy as np
import ml_dtypes
import concourse.bass as bass
import concourse.mybir as mybir
from concourse.bass_utils import run_bass_kernel_spmd
from contextlib import ExitStack

F32 = mybir.dt.float32
BF16 = mybir.dt.bfloat16
AF = mybir.ActivationFunctionType
ALU = mybir.AluOpType
AX = mybir.AxisListType.X

D, DI, DS, FF = 1024, 2048, 64, 4096
TPC, HALO = 2048, 3
NCH, CH = 4, 512
NECH, ECH = 4, 512
EPS = 1e-5

THS = (0.24419428096993973, 0.46396913384288541, 0.68618592952553059,
       0.90596078239847633, 1.0839784132255623)

ENGS = ("sync", "scalar", "vector", "tensor", "gpsimd")
DMAQ = ("sync", "scalar", "vector", "gpsimd")


class _Op:
    __slots__ = ("eng", "fn", "waits", "hard", "inc", "val", "need", "dma")

    def __init__(self, eng, fn, waits, hard, dma):
        self.eng, self.fn = eng, fn
        self.dma = dma
        self.inc = 16 if dma else 1
        self.waits = [w for w in waits if w is not None]
        self.hard = [w for w in hard if w is not None]
        self.val = None
        self.need = False


class Sched:
    def __init__(self):
        self.prog = {e: [] for e in ENGS}

    def add(self, eng, fn, waits=(), hard=(), dma=False):
        op = _Op(eng, fn, waits, hard, dma)
        self.prog[eng].append(op)
        return op

    def finalize(self):
        for e in ENGS:
            for op in self.prog[e]:
                for w in op.waits:
                    w.need = True
                for w in op.hard:
                    w.need = True
        for e in ENGS:
            cc, cd = 0, 0
            for op in self.prog[e]:
                if op.dma:
                    cd += op.inc
                    op.val = cd
                elif op.need:
                    cc += op.inc
                    op.val = cc

    def emitter(self, e, sems):
        def run(eng):
            waited = {}
            last_dma = None
            for op in self.prog[e]:
                if op.dma and last_dma is not None and \
                        waited.get((e, 'd'), -1) < last_dma:
                    eng.wait_ge(sems[(e, 'd')], last_dma)
                    waited[(e, 'd')] = last_dma
                for w in op.waits:
                    k = (w.eng, 'd' if w.dma else 'c')
                    if waited.get(k, -1) >= w.val:
                        continue
                    eng.wait_ge(sems[k], w.val)
                    waited[k] = w.val
                for w in op.hard:
                    k = (w.eng, 'd' if w.dma else 'c')
                    if waited.get(k, -1) >= w.val:
                        continue
                    eng.wait_ge(sems[k], w.val)
                    waited[k] = w.val
                ins = op.fn(eng)
                if op.val is not None:
                    ins.then_inc(sems[(e, 'd' if op.dma else 'c')], op.inc)
                    if op.dma:
                        last_dma = op.val
        return run


def _build(debug=False):
    nc = bass.Bass("TRN2", target_bir_lowering=False)
    S = Sched()

    x_in = nc.dram_tensor("x_in", [HALO + TPC, D], F32, kind="ExternalInput")
    w_in = nc.dram_tensor("w_in", [D, 2 * DI], F32, kind="ExternalInput")
    w_dt = nc.dram_tensor("w_dt", [DI, DI], F32, kind="ExternalInput")
    w_x = nc.dram_tensor("w_x", [DI, 2 * DS], F32, kind="ExternalInput")
    w_out = nc.dram_tensor("w_out", [DI, D], F32, kind="ExternalInput")
    w_fc = nc.dram_tensor("w_fc", [FF, D], F32, kind="ExternalInput")
    w_pr = nc.dram_tensor("w_pr", [D, FF], F32, kind="ExternalInput")
    lngT = nc.dram_tensor("lngT", [128, 8], F32, kind="ExternalInput")
    lnbT = nc.dram_tensor("lnbT", [128, 8], F32, kind="ExternalInput")
    convw = nc.dram_tensor("convw", [128, 64], F32, kind="ExternalInput")
    convb = nc.dram_tensor("convb", [128, 16], F32, kind="ExternalInput")
    dvec = nc.dram_tensor("dvec", [128, 16], F32, kind="ExternalInput")
    bdt = nc.dram_tensor("bdt", [128, 16], F32, kind="ExternalInput")
    alog = nc.dram_tensor("alog", [DS, 1], F32, kind="ExternalInput")
    selv = nc.dram_tensor("selv", [DS, 8], F32, kind="ExternalInput")
    maskv = nc.dram_tensor("maskv", [128, 1], F32, kind="ExternalInput")
    identf = nc.dram_tensor("identf", [128, 128], F32, kind="ExternalInput")
    out_t = nc.dram_tensor("out", [TPC, D], F32, kind="ExternalOutput")

    sc_u = nc.dram_tensor("sc_u", [128, 16, TPC], F32)
    sc_sres = nc.dram_tensor("sc_sres", [128, 16, TPC], F32)
    sc_cnt = nc.dram_tensor("sc_cnt", [TPC, D], BF16)
    wfc_bf = nc.dram_tensor("wfc_bf", [FF, D], BF16)
    wpr_bf = nc.dram_tensor("wpr_bf", [D, FF], BF16)
    wout_d = nc.dram_tensor("wout_d", [DI, D], BF16)
    sc_dl = nc.dram_tensor("sc_dl", [1, TPC], F32)
    sc_ys = nc.dram_tensor("sc_ys", [1, TPC], F32)
    cc_in = nc.dram_tensor("cc_in", [1, 2 * DS], F32)
    cc_out = nc.dram_tensor("cc_out", [8, 2 * DS], F32)

    dbg = {}
    if debug:
        for nm, shp in (("u", [128, 16, TPC]),
                        ("sres", [128, 16, TPC]), ("cnt_", [TPC, D]),
                        ("bc", [128, TPC]), ("delta", [1, TPC]),
                        ("ys", [1, TPC]), ("hgt", [2, DS]),
                        ("ad", [DS, TPC]), ("hs1", [DS, TPC]),
                        ("hs2", [DS, TPC]), ("cy", [128, TPC // 2])):
            dt = BF16 if nm == "cnt_" else F32
            dbg[nm] = nc.dram_tensor("dbg_" + nm, shp, dt, kind="ExternalOutput")

    with ExitStack() as ctx:
        E = ctx.enter_context

        G = [E(nc.sbuf_tensor(f"g{i}", [128, 2048], F32)) for i in range(25)]

        def gh(i, h):
            return G[i][:, h * 1024:(h + 1) * 1024]

        def gq(i, q):
            return G[i][:, q * 512:(q + 1) * 512]

        def gbf(i):
            return G[i][:].bitcast(BF16)

        wx_sb = E(nc.sbuf_tensor("wx_sb", [128, 16 * 128], BF16))
        ct = E(nc.sbuf_tensor("ct", [128, 8], F32))
        ones_c = E(nc.sbuf_tensor("ones_c", [128, 1], BF16))
        bdtc = E(nc.sbuf_tensor("bdtc", [128, 16], F32))
        idf = E(nc.sbuf_tensor("idf", [64, 64], F32))
        idb = E(nc.sbuf_tensor("idb", [128, 128], BF16))
        cw = E(nc.sbuf_tensor("cw", [128, 64], F32))
        cb = E(nc.sbuf_tensor("cb", [128, 16], F32))
        dv = E(nc.sbuf_tensor("dv", [128, 16], F32))
        msk = E(nc.sbuf_tensor("msk", [128, 1], F32))
        xn2h = E(nc.sbuf_tensor("xn2h", [128, 8 * HALO], BF16))
        hist = E(nc.sbuf_tensor("hist", [128, 48], F32))
        a_col = E(nc.sbuf_tensor("a_col", [DS, 1], F32))
        pend = E(nc.sbuf_tensor("pend", [DS, 8], F32))
        hg = E(nc.sbuf_tensor("hg", [DS, 2], F32))
        hgT = E(nc.sbuf_tensor("hgT", [2, DS], F32))
        ag = E(nc.sbuf_tensor("ag", [8, 2 * DS], F32))
        agH = E(nc.sbuf_tensor("agH", [DS, 8], F32))
        agG = E(nc.sbuf_tensor("agG", [DS, 8], F32))
        dtile = E(nc.sbuf_tensor("dtile", [DS, 8], F32))
        dcur = E(nc.sbuf_tensor("dcur", [DS, 1], F32))
        hin = E(nc.sbuf_tensor("hin", [DS, 1], F32))
        sel_sb = E(nc.sbuf_tensor("sel_sb", [DS, 8], F32))
        astat = E(nc.sbuf_tensor("astat", [128, 64], F32))
        pstat = E(nc.sbuf_tensor("pstat", [128, 51], F32))
        bvb = E(nc.sbuf_tensor("bvb", [128, 16], F32))
        bva = E(nc.sbuf_tensor("bva", [128, 16], F32))
        lngc = E(nc.sbuf_tensor("lngc", [128, 8], F32))
        lnbc = E(nc.sbuf_tensor("lnbc", [128, 8], F32))
        lnb_bf = E(nc.sbuf_tensor("lnb_bf", [128, 8], BF16))

        PS = [E(nc.psum_tensor(f"ps{i}", [128, 512], F32)) for i in range(8)]
        PSB = [PS[6][:].bitcast(BF16), PS[7][:].bitcast(BF16)]

        sems = {}
        for e in ENGS:
            sems[(e, 'c')] = E(nc.semaphore("c_" + e))
        for e in DMAQ:
            sems[(e, 'd')] = E(nc.semaphore("d_" + e))

        xn2f = [gh(i // 2, i % 2).bitcast(BF16) for i in range(8)]
        xnbf = xn2f
        winA = [gh(4 + i // 2, i % 2).bitcast(BF16) for i in range(8)]
        winB = [gh(8 + i // 2, i % 2).bitcast(BF16) for i in range(8)]
        wdt_sb = [gh(12 + i // 2, i % 2).bitcast(BF16) for i in range(16)]
        wfcT = [gbf(4 + i) for i in range(8)]
        wprT = [gbf(12 + i)[:, j * 1024:(j + 1) * 1024]
                for i in range(8) for j in range(4)]
        xcbf = [gbf(20)[:, i * 512:(i + 1) * 512] for i in range(8)] + \
               [gbf(21)[:, i * 512:(i + 1) * 512] for i in range(8)]
        htile = [gbf(20 + i // 8)[:, (i % 8) * 512:(i % 8 + 1) * 512]
                 for i in range(32)]
        y2t2 = [[gbf(4)[:, i * 512:(i + 1) * 512] for i in range(8)] +
                [gbf(5)[:, i * 512:(i + 1) * 512] for i in range(8)],
                xcbf]
        bst_bf = G[22][:].bitcast(BF16)[0:64, 0:2048]
        cst_bf = G[22][:].bitcast(BF16)[0:64, 2048:4096]
        ysb = G[10]
        ad_sb = G[24][0:64, :]
        hsl = G[23][0:64, :]
        cy_bf = G[24][:].bitcast(BF16)[0:64, 0:2048]
        wout_sb = [gbf(6 + i // 4)[:, (i % 4) * 1024:(i % 4 + 1) * 1024]
                   for i in range(16)]
        cntT = [G[22][:].bitcast(BF16)[:, i * 1024:(i + 1) * 1024]
                for i in range(4)]
        rl_b = [gq(24, 0), gq(24, 1)]
        cnt_b = [gq(24, 2), gq(24, 3)]
        xp_b = [G[23][:, 0:1024], G[23][:, 1533:2048]]
        acc_b = [gq(24, 0), gq(24, 1)]
        xcf_b = [gq(24, 2), gq(24, 3)]
        u_b = acc_b
        sp_f = acc_b[0]
        sp_g2 = [acc_b[1].bitcast(BF16)[:, 0:512],
                 acc_b[1].bitcast(BF16)[:, 512:1024]]
        sre_b = [gq(22, j) for j in range(4)] + \
            [gq(23, j) for j in range(4)]
        fub = [G[0], G[1]]
        fsrb = [G[2], G[3]]
        f_xn = [gh(16, 0), gh(18, 0)]
        f_x4 = [gh(16, 1), gh(18, 1), gh(14, 0), gh(14, 1)]
        f_s4 = [gh(17, 0), gh(12, 0), gh(15, 0), gh(15, 1)]
        f_g2 = [gh(17, 1), gh(12, 1)]
        f_sg2 = [gh(19, 0), gh(13, 0)]
        f_o2 = [gh(19, 1), gh(13, 1)]
        f_t9 = [gq(24, 0), gq(24, 1)]
        dtmp = G[23][0:64, 0:512]
        drow = G[23][0:1, 515:1027]
        dsel = pend

        def dma(q, out, in_, waits=(), **kw):
            return S.add(q, lambda e, o=out, i=in_, k=kw: e.dma_start(o, i, **k),
                         waits=waits, dma=True)

        def op(eng, fn, waits=(), hard=()):
            return S.add(eng, fn, waits=waits, hard=hard)

        c_eps = ct[:, 0:1]
        c_one = ct[:, 1:2]
        c_mh = ct[:, 2:3]
        c_p2 = ct[:, 4:5]

        t_ct = op("vector", lambda e: e.memset(ct[:, 0:8], 0.0))
        for col, val in ((0, EPS), (1, 1.0), (2, -0.5), (4, 0.2)):
            t_ct = op("vector", lambda e, c=col, v=val: e.memset(ct[:, c:c + 1], v),
                      waits=(t_ct,))
        t_onec = op("vector", lambda e: e.memset(ones_c[:, :], 1.0), waits=(t_ct,))

        ld = {}
        for nm, dst, src in (("cw", cw, convw), ("cb", cb, convb), ("dv", dv, dvec),
                             ("msk", msk, maskv),
                             ("alog", a_col, alog), ("sel", sel_sb, selv),
                             ("bdt", bdtc, bdt), ("lng", lngc, lngT),
                             ("lnb", lnbc, lnbT)):
            ld[nm] = dma("sync", dst[:], src.ap())
        t_idb = dma("gpsimd", idb[:], identf.ap())
        ld["idf"] = dma("sync", idf[:], identf.ap()[0:64, 0:64])
        t_ac = op("scalar", lambda e: e.activation(a_col[:], a_col[:], AF.Exp),
                  waits=(ld["alog"],))
        t_ac = op("vector", lambda e: e.tensor_single_scalar(a_col[:], a_col[:],
                                                             -1.0, ALU.mult),
                  waits=(t_ac,))

        t_winA = [dma("gpsimd", winA[i], w_in.ap()[128 * i:128 * (i + 1), DI:2 * DI])
                  for i in range(8)]
        t_winB = [dma("gpsimd", winB[i], w_in.ap()[128 * i:128 * (i + 1), 0:DI])
                  for i in range(8)]
        t_wdt = [dma("gpsimd", wdt_sb[k], w_dt.ap()[128 * k:128 * (k + 1), :])
                 for k in range(16)]
        t_wx = [dma("gpsimd", wx_sb[:, 128 * k:128 * (k + 1)],
                    w_x.ap()[128 * k:128 * (k + 1), :]) for k in range(16)]
        t_wfcc = [dma("gpsimd", wfc_bf.ap()[1024 * i:1024 * (i + 1), :],
                      w_fc.ap()[1024 * i:1024 * (i + 1), :]) for i in range(4)]

        xa = [gh(20, 0), gh(20, 1), gh(21, 1)]
        xnbf2 = [G[21][:, 0:512].bitcast(BF16), G[21][:, 512:1024].bitcast(BF16)]

        last_x_use = [(), ()]
        prevt = [{}, {}]
        xnbf_free = [(), ()]
        tr_evac = [None, None]
        a_evac = {}
        muop = [None] * 17
        rop = [None] * 17
        a_tail = []
        sre_store2 = [None, None]
        c1_last_mm = None
        c1q = []
        ps_tok = [None] * 8
        s2op = [None] * 17

        def c1_group(c, g):
            nonlocal c1_last_mm
            evs = []
            for fo in range(4 * g, 4 * g + 4):
                b = fo % 4
                mm = None
                for k in range(8):
                    w0 = (t_was[k],)
                    if k == 0:
                        w0 += (ps_tok[b], a_evac[4 * c + 3])
                    mm = op("tensor",
                            lambda e, ps=PS[b],
                            w=winA[k][:, 128 * fo:128 * (fo + 1)],
                            r=xn2f[k][:, CH * c:CH * (c + 1)],
                            s=(k == 0), t=(k == 7):
                            e.matmul(ps[:, 0:512], w, r, start=s, stop=t),
                            waits=w0)
                c1_last_mm = mm
                t_ev = op("scalar", lambda e, ps=PS[b],
                          o=sre_b[4 * (g % 2) + fo % 4],
                          bi=bva[:, fo:fo + 1]:
                          e.activation(o, ps[:, 0:512], AF.Silu, bias=bi),
                          waits=(mm, sre_store2[g % 2], t_bva))
                ps_tok[b] = t_ev
                evs.append(t_ev)
            sre_store2[g % 2] = dma(
                "scalar",
                sc_sres.ap()[:, 4 * g:4 * g + 4, CH * c:CH * (c + 1)],
                G[22 + g % 2][:, 0:2048], waits=tuple(evs))

        def a_tail_emit(ti, p, t_xnn):
            nonlocal tr_evac
            t_tr = None
            for d8 in range(8):
                q = d8 % 2
                nrow = HALO if ti < 0 else 128
                pbv = PSB[q][0:128, 0:nrow]
                t_tr = op("tensor",
                          lambda e, i_=xnbf2[p][0:nrow, 128 * d8:128 * (d8 + 1)],
                          o=pbv, idn=idb[0:nrow, 0:nrow]:
                          e.transpose(o, i_, idn),
                          waits=(t_xnn, t_idb, tr_evac[q]))
                if ti < 0:
                    dst = xn2h[:, HALO * d8:HALO * (d8 + 1)]
                else:
                    dst = xn2f[d8][:, 128 * ti:128 * (ti + 1)]
                t_ev = op("vector", lambda e, i_=pbv, o=dst: e.tensor_copy(o, i_),
                          waits=(t_tr,))
                tr_evac[q] = t_ev
                a_evac[ti] = t_ev
            xnbf_free[p] = (t_tr,)
            if ti >= 14:
                a_tail.append(t_tr)

        pend_tail = None
        xuse = [(), (), ()]
        for ti in range(-1, 16):
            if pend_tail is not None:
                a_tail_emit(*pend_tail)
            p = ti % 2
            a3 = ti % 3
            nrow = HALO if ti < 0 else 128
            r0 = 0 if ti < 0 else HALO + 128 * ti
            pc = 16 if ti < 0 else ti
            xt = xa[a3][0:nrow, :]
            pv = prevt[p]
            if ti == 1:
                t_lnbbf = op("vector", lambda e: e.tensor_copy(lnb_bf[:], lnbc[:]),
                             waits=(ld["lnb"],))
                bvb_mm = None
                for fo in range(16):
                    for k in range(8):
                        bvb_mm = op("tensor", lambda e, w=winB[k][:, 128 * fo:128 * (fo + 1)],
                                    r=lnb_bf[:, k:k + 1], s=(k == 0), t=(k == 7), f=fo:
                                    e.matmul(PS[2][:, f:f + 1], w, r, start=s, stop=t),
                                    waits=(t_winB[k], t_lnbbf))
                t_bvb = op("vector", lambda e: e.tensor_copy(bvb[:], PS[2][:, 0:16]),
                           waits=(bvb_mm,))
                bva_mm = None
                for fo in range(16):
                    for k in range(8):
                        bva_mm = op("tensor", lambda e, w=winA[k][:, 128 * fo:128 * (fo + 1)],
                                    r=lnb_bf[:, k:k + 1], s=(k == 0), t=(k == 7), f=fo:
                                    e.matmul(PS[3][:, f:f + 1], w, r, start=s, stop=t),
                                    waits=(t_winA[k], t_lnbbf))
                t_bva = op("vector", lambda e: e.tensor_copy(bva[:], PS[3][:, 0:16]),
                           waits=(bva_mm,))
                t_wbs = [op("vector", lambda e, w=winB[k2], sc=lngc[:, k2:k2 + 1]:
                            e.tensor_scalar_mul(w, w, sc),
                            waits=(bvb_mm, ld["lng"], t_winB[k2])) for k2 in range(8)]
                t_was = [op("vector", lambda e, w=winA[k2], sc=lngc[:, k2:k2 + 1]:
                            e.tensor_scalar_mul(w, w, sc),
                            waits=(bva_mm, ld["lng"], t_winA[k2])) for k2 in range(8)]
                ps_tok[2] = t_bvb
                ps_tok[3] = t_bva

            def sc_(c, n=nrow, p_=p):
                return astat[0:n, 32 * p_ + c:32 * p_ + c + 1]

            def sg_(c0, c1, n=nrow, p_=p):
                return astat[0:n, 32 * p_ + c0:32 * p_ + c1]

            t_x = dma("sync", xt, x_in.ap()[r0:r0 + nrow, :], waits=xuse[a3])
            t_b1 = op("vector", lambda e, a=xt[:, 0:512], o=sg_(8, 14):
                      e.bn_stats(o, a), waits=(t_x, pv.get("ba")))
            t_b2 = op("vector", lambda e, a=xt[:, 512:1024], o=sg_(14, 20):
                      e.bn_stats(o, a), waits=(t_x, t_b1))
            t_ba = op("vector", lambda e, i_=sg_(8, 20), o=sg_(0, 2):
                      e.bn_aggr(o, i_),
                      waits=(t_b2, pv.get("muc"), pv.get("xnn"),
                             pv.get("l"), pv.get("d")))
            t_muc = op("vector", lambda e, i_=sc_(0), o=pstat[0:nrow, pc:pc + 1]:
                       e.tensor_copy(o, i_), waits=(t_ba,))
            muop[pc] = t_ba
            t_l = op("scalar", lambda e, a=sc_(1), o=sc_(2), bi=c_eps[0:nrow, :]:
                     e.activation(o, a, AF.Ln, bias=bi),
                     waits=(t_ba, t_ct, pv.get("r")))
            t_r = op("scalar", lambda e, a=sc_(2),
                     o=pstat[0:nrow, 17 + pc:18 + pc], sc=c_mh[0:nrow, :]:
                     e.activation(o, a, AF.Exp, scale=sc), waits=(t_l,))
            rop[pc] = t_r
            t_rr = op("vector", lambda e, r=pstat[0:nrow, 17 + pc:18 + pc],
                      o=sc_(3):
                      e.tensor_tensor(o, r, r, ALU.mult),
                      waits=(t_r, pv.get("d")))
            t_d = op("vector", lambda e, a=sc_(1), b=sc_(3), o=sc_(4):
                     e.tensor_tensor(o, a, b, ALU.mult),
                     waits=(t_rr, pv.get("l2"), pv.get("s2")))
            t_s2 = op("vector", lambda e, a=sc_(4),
                      o=pstat[0:nrow, 34 + pc:35 + pc]:
                      e.tensor_single_scalar(o, a, EPS, ALU.add), waits=(t_d,))
            s2op[pc] = t_s2
            t_l2 = op("scalar", lambda e, a=sc_(4), o=sc_(5), bi=c_eps[0:nrow, :]:
                      e.activation(o, a, AF.Ln, bias=bi),
                      waits=(t_d, pv.get("r2")))
            t_r2 = op("scalar", lambda e, a=sc_(5), o=sc_(6), sc=c_mh[0:nrow, :]:
                      e.activation(o, a, AF.Exp, scale=sc),
                      waits=(t_l2, pv.get("rc")))
            t_rc = op("vector", lambda e, a=sc_(6),
                      r=pstat[0:nrow, 17 + pc:18 + pc], o=sc_(7):
                      e.tensor_tensor(o, a, r, ALU.mult),
                      waits=(t_r2, pv.get("xnn")))
            t_xnn = op("vector", lambda e, a=xt, o=xnbf2[p][0:nrow, 0:1024],
                       m=sc_(0), r=sc_(7):
                       e.tensor_scalar(o, a, m, r, ALU.subtract, ALU.mult),
                       waits=(t_rc,) + xnbf_free[p], hard=(t_ba,))
            xuse[a3] = (t_xnn,)
            prevt[p] = dict(ba=t_ba, muc=t_muc, l=t_l, r=t_r, d=t_d, s2=t_s2,
                            l2=t_l2, r2=t_r2, rc=t_rc, xnn=t_xnn)
            if ti >= 14:
                a_tail += [t_xnn]
            pend_tail = (ti, p, t_xnn)
            if ti >= 0 and ti % 4 == 3:
                for g in range(4):
                    c1q.append((ti // 4, g))
            if ti >= 4 and c1q:
                c1_group(*c1q.pop(0))
        a_tail_emit(*pend_tail)
        while c1q:
            c1_group(*c1q.pop(0))

        t_wprc = [dma("gpsimd", wpr_bf.ap()[256 * i:256 * (i + 1), :],
                      w_pr.ap()[256 * i:256 * (i + 1), :],
                      waits=(a_evac[15],) if i == 0 else ())
                  for i in range(4)]
        ps_tok[6] = tr_evac[0]
        ps_tok[7] = tr_evac[1]
        a_guard = tuple(t for t in a_tail + [tr_evac[0], tr_evac[1],
                                             sre_store2[0], sre_store2[1]]
                        if t is not None)
        sre_tok = [sre_store2[0], sre_store2[1]]

        t_wfcT = [None] * 8
        for i in range(4):
            t_wfcT[i] = dma("scalar", wfcT[i],
                            wfc_bf.ap()[:, 128 * i:128 * (i + 1)],
                            waits=(c1_last_mm,) + tuple(t_wfcc), transpose=True)

        t_xc = [None] * 16
        xc_readers = [None] * 16
        hist_tok = None
        drow_tok = None
        t_dl_dmas = []
        c2_last_mm = None
        c2_last_wdt = None
        u_tok = [None, None]
        sp_tok = [None, None]
        acc_tok = [None, None]
        xcf_tok = [None, None]
        xp_tok = [None, None]

        for c in range(NCH):
            for fo in range(16):
                b = fo % 2
                bk = fo % 3
                if c == 0:
                    mmh = None
                    for k in range(8):
                        w0 = (t_wbs[k], ps_tok[3], a_evac[-1]) if k == 0 \
                            else (t_wbs[k],)
                        mmh = op("tensor",
                                 lambda e, ps=PS[3],
                                 w=winB[k][:, 128 * fo:128 * (fo + 1)],
                                 r=xn2h[:, HALO * k:HALO * (k + 1)],
                                 s=(k == 0), t=(k == 7):
                                 e.matmul(ps[:, 0:HALO], w, r, start=s, stop=t),
                                 waits=w0)
                    t_h = op("vector",
                             lambda e, ps=PS[3], o=hist[:, 3 * fo:3 * fo + 3],
                             bi=bvb[:, fo:fo + 1]:
                             e.tensor_scalar(o, ps[:, 0:HALO], bi, msk[:, 0:1],
                                             ALU.add, ALU.mult),
                             waits=(mmh, ld["msk"], hist_tok, t_bvb))
                    ps_tok[3] = t_h
                    hist_tok = t_h
                mm = None
                for k in range(8):
                    w0 = (t_wbs[k], ps_tok[bk]) if k == 0 else (t_wbs[k],)
                    mm = op("tensor",
                            lambda e, ps=PS[bk], w=winB[k][:, 128 * fo:128 * (fo + 1)],
                            r=xn2f[k][:, CH * c:CH * (c + 1)], s=(k == 0), t=(k == 7):
                            e.matmul(ps[:, 0:512], w, r, start=s, stop=t), waits=w0)
                c2_last_mm = mm
                t_ev = op("scalar", lambda e, ps=PS[bk], o=xp_b[b][:, 3:515],
                          bi=bvb[:, fo:fo + 1]:
                          e.activation(o, ps[:, 0:512], AF.Identity, bias=bi),
                          waits=(mm, xp_tok[b], t_bvb) +
                          (a_guard if c == 0 and fo < 2 else ()))
                ps_tok[bk] = t_ev
                t_hi = op("vector",
                          lambda e, o=xp_b[b][:, 0:3], i_=hist[:, 3 * fo:3 * fo + 3]:
                          e.tensor_copy(o, i_), waits=(hist_tok, t_ev))
                t_a0 = op("vector", lambda e, o=acc_b[b], i_=xp_b[b][:, 0:512],
                          s_=cw[:, 4 * fo:4 * fo + 1], bi=cb[:, fo:fo + 1]:
                          e.tensor_scalar(o, i_, s_, bi, ALU.mult, ALU.add),
                          waits=(t_ev, t_hi, ld["cw"], ld["cb"], acc_tok[b],
                                 sp_tok[0], sp_tok[1]))
                t_a1 = op("vector", lambda e, o=acc_b[b], i_=xp_b[b][:, 1:513],
                          s_=cw[:, 4 * fo + 1:4 * fo + 2], a2=acc_b[b]:
                          e.scalar_tensor_tensor(o, i_, s_, a2, ALU.mult, ALU.add),
                          waits=(t_a0,))
                t_a2 = op("vector", lambda e, o=acc_b[b], i_=xp_b[b][:, 2:514],
                          s_=cw[:, 4 * fo + 2:4 * fo + 3], a2=acc_b[b]:
                          e.scalar_tensor_tensor(o, i_, s_, a2, ALU.mult, ALU.add),
                          waits=(t_a1,))
                t_a3 = op("vector", lambda e, o=acc_b[b], i_=xp_b[b][:, 3:515],
                          s_=cw[:, 4 * fo + 3:4 * fo + 4], a2=acc_b[b]:
                          e.scalar_tensor_tensor(o, i_, s_, a2, ALU.mult, ALU.add),
                          waits=(t_a2,))
                t_hs = op("vector", lambda e, o=hist[:, 3 * fo:3 * fo + 3],
                          i_=xp_b[b][:, 512:515]:
                          e.tensor_copy(o, i_), waits=(t_ev, t_hi))
                hist_tok = t_hs
                xp_tok[b] = t_a3
                t_si = op("scalar", lambda e, o=xcf_b[b], i_=acc_b[b]:
                          e.activation(o, i_, AF.Silu),
                          waits=(t_a3, xcf_tok[b]))
                acc_tok[b] = t_si
                t_uo = op("vector", lambda e, o=u_b[b], i_=xcf_b[b], s_=dv[:, fo:fo + 1]:
                          e.tensor_scalar_mul(o, i_, s_),
                          waits=(t_si, ld["dv"], u_tok[b]))
                t_ud = dma("sync", sc_u.ap()[:, fo, CH * c:CH * (c + 1)], u_b[b],
                           waits=(t_uo,))
                u_tok[b] = t_ud
                acc_tok[b] = t_ud
                t_xbf = op("vector", lambda e, o=xcbf[fo], i_=xcf_b[b]:
                           e.tensor_copy(o, i_), waits=(t_si, xc_readers[fo]) +
                           (a_guard if c == 0 else ()))
                xcf_tok[b] = t_xbf
                t_xc[fo] = t_xbf
            mmb = None
            for k in range(16):
                w0 = (t_xc[k], t_wx[k], ps_tok[4]) if k == 0 else (t_xc[k], t_wx[k])
                mmb = op("tensor",
                         lambda e, ps=PS[4], w=wx_sb[:, 128 * k:128 * k + 64],
                         r=xcbf[k], s=(k == 0), t=(k == 15):
                         e.matmul(ps[0:64, 0:512], w, r, start=s, stop=t), waits=w0)
            mmc = None
            for k in range(16):
                w0 = (t_xc[k], t_wx[k], ps_tok[3]) if k == 0 else (t_xc[k], t_wx[k])
                mmc = op("tensor",
                         lambda e, ps=PS[3], w=wx_sb[:, 128 * k + 64:128 * (k + 1)],
                         r=xcbf[k], s=(k == 0), t=(k == 15):
                         e.matmul(ps[0:64, 0:512], w, r, start=s, stop=t), waits=w0)
            t_bc = op("vector", lambda e, ps=PS[4], o=bst_bf[:, CH * c:CH * (c + 1)]:
                      e.tensor_copy(o, ps[0:64, 0:512]), waits=(mmb, ps_tok[4]) +
                      (a_guard if c == 0 else ()))
            ps_tok[4] = t_bc
            t_cc2 = op("vector", lambda e, ps=PS[3], o=cst_bf[:, CH * c:CH * (c + 1)]:
                       e.tensor_copy(o, ps[0:64, 0:512]), waits=(mmc,))
            ps_tok[3] = t_cc2
            for k in range(16):
                xc_readers[k] = mmc
            sp_mm = None
            lns = [None] * 16
            spmm_tok = [None, None]

            def emit_spmm(co2):
                nonlocal sp_mm
                w0 = (lns[co2], t_onec) + ((ps_tok[3],) if co2 == 0 else ())
                sp_mm = op("tensor", lambda e, g=sp_g2[co2 % 2],
                           s=(co2 == 0), t=(co2 == 15):
                           e.matmul(PS[3][0:1, 0:512], ones_c[:], g,
                                    start=s, stop=t), waits=w0)
                spmm_tok[co2 % 2] = sp_mm

            for co in range(16):
                b = 5 + co % 3
                if co >= 2:
                    emit_spmm(co - 2)
                mm = None
                for k in range(16):
                    w0 = ((t_xc[k], t_wdt[k], ps_tok[b]) if k == 0
                          else (t_xc[k], t_wdt[k]))
                    mm = op("tensor",
                            lambda e, ps=PS[b],
                            l=wdt_sb[k][:, 128 * co:128 * (co + 1)],
                            r=xcbf[k], s=(k == 0), t=(k == 15):
                            e.matmul(ps[:, 0:512], l, r, start=s, stop=t),
                            waits=w0)
                c2_last_wdt = mm
                for k in range(16):
                    xc_readers[k] = mm
                t_exo = op("scalar", lambda e, ps=PS[b], bi=bdtc[:, co:co + 1]:
                           e.activation(sp_f, ps[:, 0:512], AF.Exp, bias=bi),
                           waits=(mm, ld["bdt"], sp_tok[0]) +
                           ((u_tok[0], u_tok[1]) if co == 0 else ()))
                ps_tok[b] = t_exo
                t_ln = op("scalar", lambda e, g=sp_g2[co % 2]:
                          e.activation(g, sp_f, AF.Ln, bias=c_one),
                          waits=(t_exo, spmm_tok[co % 2]))
                sp_tok[0] = t_ln
                lns[co] = t_ln
                sp_tok[1] = t_ln
            emit_spmm(14)
            emit_spmm(15)
            sp_tok[1] = sp_mm
            t_de = op("vector", lambda e:
                      e.tensor_single_scalar(drow, PS[3][0:1, 0:512],
                                             1.0 / DI, ALU.mult),
                      waits=(sp_mm, drow_tok) + (a_guard if c == 0 else ()))
            ps_tok[3] = t_de
            t_dd = dma("scalar", sc_dl.ap()[0:1, CH * c:CH * (c + 1)], drow,
                       waits=(t_de,))
            drow_tok = t_dd
            t_dl_dmas.append(t_dd)

        t_ad = [None] * 4
        t_pe = [None] * 4
        prev_dl = ()
        for j in range(4):
            t_dl = dma("gpsimd", dtmp[:, :],
                       sc_dl.ap()[0:1, CH * j:CH * (j + 1)].partition_broadcast(DS),
                       waits=(t_dl_dmas[j], xp_tok[0], xp_tok[1],
                              sre_tok[0], sre_tok[1]) + prev_dl)
            t_ad[j] = op("scalar",
                         lambda e, o=ad_sb[:, CH * j:CH * (j + 1)]:
                         e.activation(o, dtmp, AF.Exp, scale=a_col[:]),
                         waits=(t_dl, t_ac, sp_tok[0], sp_tok[1],
                                acc_tok[0], acc_tok[1], xcf_tok[0], xcf_tok[1],
                                u_tok[0], u_tok[1]))
            t_pe[j] = op("vector", lambda e, o=pend[:, j:j + 1]:
                         e.reduce_sum(o, dtmp, axis=AX), waits=(t_dl,))
            prev_dl = (t_ad[j], t_pe[j])
        t_pt = op("vector", lambda e: e.reduce_sum(pend[:, 4:5], pend[:, 0:4], axis=AX),
                  waits=tuple(t_pe))
        t_dbgad = None
        if debug:
            t_dbgad = dma("gpsimd", dbg["ad"].ap(), ad_sb, waits=tuple(t_ad))
        t_sc1 = op("vector", lambda e: e.tensor_tensor_scan(
            hsl, ad_sb, bst_bf, 0.0, ALU.mult, ALU.add),
            waits=tuple(t_ad) + (ps_tok[4], u_tok[0], u_tok[1],
                                 sp_tok[0], sp_tok[1], xp_tok[0], xp_tok[1],
                                 sre_tok[0], sre_tok[1], drow_tok))
        t_hg0 = op("vector", lambda e: e.tensor_copy(hg[:, 0:1], hsl[:, TPC - 1:TPC]),
                   waits=(t_sc1,))
        t_dbgh1 = None
        if debug:
            t_dbgh1 = dma("gpsimd", dbg["hs1"].ap(), hsl, waits=(t_sc1,))
        t_hg1 = op("scalar", lambda e: e.activation(hg[:, 1:2], a_col[:], AF.Exp,
                                                    scale=pend[:, 4:5]),
                   waits=(t_pt, t_ac))
        t_hgt = op("tensor", lambda e: e.transpose(PS[2][0:2, 0:64], hg[:], idf[:]),
                   waits=(t_hg0, t_hg1, ps_tok[2], ld["idf"]))
        t_hge = op("vector", lambda e: e.tensor_copy(hgT[:], PS[2][0:2, 0:64]),
                   waits=(t_hgt,))
        ps_tok[2] = t_hge
        t_ci0 = dma("gpsimd", cc_in.ap()[0:1, 0:DS], hgT[0:1, :], waits=(t_hge,))
        t_ci1 = dma("gpsimd", cc_in.ap()[0:1, DS:2 * DS], hgT[1:2, :], waits=(t_hge,))
        t_cc = S.add("gpsimd", lambda e: e.collective_compute(
            "AllGather", ALU.bypass, replica_groups=[list(range(8))],
            ins=[cc_in.ap().opt()], outs=[cc_out.ap().opt()]),
            waits=(t_ci0, t_ci1), dma=False)
        t_ag = dma("gpsimd", ag[:], cc_out.ap(), waits=(t_cc,))
        t_agt = op("tensor", lambda e: e.transpose(PS[2][0:64, 0:8], ag[:, 0:DS],
                                                   idf[0:8, 0:8]),
                   waits=(t_ag, ps_tok[2], ld["idf"]))
        t_agh = op("vector", lambda e: e.tensor_copy(agH[:], PS[2][0:64, 0:8]),
                   waits=(t_agt,))
        t_agt2 = op("tensor", lambda e: e.transpose(PS[2][0:64, 8:16],
                                                    ag[:, DS:2 * DS], idf[0:8, 0:8]),
                    waits=(t_ag, t_agh))
        t_age = op("vector", lambda e: e.tensor_copy(agG[:], PS[2][0:64, 8:16]),
                   waits=(t_agt2,))
        ps_tok[2] = t_age
        t_prev = op("vector", lambda e: e.memset(dtile[:, 0:1], 0.0),
                    waits=(t_age, t_agh))
        for j in range(1, 8):
            if j == 4:
                t_prev = op("vector", lambda e: e.memset(dtile[:, 4:5], 0.0),
                            waits=(t_prev,))
                continue
            t_m = op("vector", lambda e, j_=j: e.tensor_tensor(
                dcur[:, 0:1], dtile[:, j_ - 1:j_], agG[:, j_ - 1:j_], ALU.mult),
                waits=(t_prev,))
            t_prev = op("vector", lambda e, j_=j: e.tensor_tensor(
                dtile[:, j_:j_ + 1], dcur[:, 0:1], agH[:, j_ - 1:j_], ALU.add),
                waits=(t_m,))
        t_dsl = op("vector", lambda e: e.tensor_tensor(dsel[:], dtile[:], sel_sb[:],
                                                       ALU.mult),
                   waits=(t_prev, ld["sel"]))
        t_hin = op("vector", lambda e: e.reduce_sum(hin[:, 0:1], dsel[:], axis=AX),
                   waits=(t_dsl,))
        t_sc2 = op("vector", lambda e: e.tensor_tensor_scan(
            ad_sb, ad_sb, bst_bf, hin[:, 0:1], ALU.mult, ALU.add),
            waits=(t_hg0, t_dbgad), hard=(t_hin,))
        t_dbgh2 = None
        if debug:
            t_dbgh2 = dma("gpsimd", dbg["hs2"].ap(), ad_sb, waits=(t_sc2,))
        t_cy = op("vector", lambda e: e.tensor_tensor(cy_bf, ad_sb, cst_bf,
                                                      ALU.mult),
                  waits=(t_sc2, t_dbgh2))
        t_dbgbc = None
        if debug:
            t_dbgbc = dma("gpsimd", dbg["bc"].ap(), G[22][:], waits=(t_cy,))
            t_dbgcy = dma("gpsimd", dbg["cy"].ap(), G[24][:, 0:1024], waits=(t_cy,))
        t_ysd = [None] * 4
        t_ysmm = None
        for j in range(4):
            t_ysmm = op("tensor", lambda e, j_=j: e.matmul(
                PS[3][0:1, 0:512], ones_c[0:64, :], cy_bf[:, 512 * j_:512 * (j_ + 1)],
                start=True, stop=True),
                waits=(t_cy, t_onec, ps_tok[3]))
            t_e = op("vector", lambda e: e.tensor_copy(dtmp[0:1, :],
                                                       PS[3][0:1, 0:512]),
                     waits=(t_ysmm, t_ysd[j - 1] if j else None)
                     + (prev_dl + (t_pe[3],) if j == 0 else ()))
            ps_tok[3] = t_e
            t_ysd[j] = dma("gpsimd", sc_ys.ap()[0:1, 512 * j:512 * (j + 1)],
                           dtmp[0:1, :], waits=(t_e,))

        t_woc = [dma("gpsimd", wout_d.ap()[1024 * i:1024 * (i + 1), :],
                     w_out.ap()[1024 * i:1024 * (i + 1), :]) for i in range(2)]

        for i in range(4, 8):
            t_wfcT[i] = dma("scalar", wfcT[i],
                            wfc_bf.ap()[:, 128 * i:128 * (i + 1)],
                            waits=(c2_last_mm,) + tuple(t_wfcc), transpose=True)
        t_wprT = [dma("scalar", wprT[j],
                      wpr_bf.ap()[:, 128 * j:128 * (j + 1)],
                      waits=(c2_last_wdt,) + tuple(t_wprc), transpose=True)
                  for j in range(32)]

        h_tok = [None] * 32
        h_readers = [None] * 32
        e_tok = [None, None]
        rl_tok = [None, None]
        cnt_tok2 = [None, None]
        e_last_mm1 = None
        e_last_mm2 = None
        t_ysmm_last = t_ysmm
        for ec in range(NECH):
            for f in range(32):
                b = f % 2
                mm = None
                for k in range(8):
                    w0 = ((t_wfcT[k], ps_tok[b]) if k == 0
                          else (t_wfcT[k],))
                    mm = op("tensor",
                            lambda e, ps=PS[b], w=wfcT[k][:, 128 * f:128 * (f + 1)],
                            r=xnbf[k][:, ECH * ec:ECH * (ec + 1)],
                            s=(k == 0), t=(k == 7):
                            e.matmul(ps[:, 0:512], w, r, start=s, stop=t), waits=w0)
                e_last_mm1 = mm
                wsq = (mm, h_readers[f])
                if ec == 0:
                    if f < 16:
                        wsq += (xc_readers[f % 16],)
                    elif f < 24:
                        wsq += (t_cy,)
                    else:
                        wsq += (tuple(t_ysd) + (t_hg0, drow_tok,
                                                xp_tok[0], xp_tok[1],
                                                sre_tok[0], sre_tok[1],
                                                t_ad[3],))
                t_mx = op("vector", lambda e, ps=PS[b], o=htile[f]:
                          e.tensor_single_scalar(o, ps[:, 0:512], 0.0, ALU.max),
                          waits=wsq)
                t_sq = op("vector", lambda e, o=htile[f]:
                          e.tensor_tensor(o, o, o, ALU.mult), waits=(t_mx,))
                ps_tok[b] = t_sq
                rl_tok[b] = t_sq
                h_tok[f] = t_sq
            for t4 in range(4):
                tg = 4 * ec + t4
                mmd = [None, None]
                for k in range(32):
                    for dn in range(2):
                        b = 2 + dn
                        w0 = (h_tok[k], t_wprT[k])
                        if k == 0:
                            w0 += (ps_tok[b],)
                        mmd[dn] = op("tensor",
                                     lambda e, ps=PS[b],
                                     l=htile[k][:, 128 * t4:128 * (t4 + 1)],
                                     r=wprT[k][:, 512 * dn:512 * (dn + 1)],
                                     s=(k == 0), t=(k == 31):
                                     e.matmul(ps[:, 0:512], l, r, start=s, stop=t),
                                     waits=w0)
                e_last_mm2 = mmd[1]
                for k in range(32):
                    h_readers[k] = mmd[1]
                for dn in range(2):
                    b = 2 + dn
                    cn = cnt_b[dn]
                    w0 = (mmd[dn], ps_tok[6 + dn])
                    if ec == 0 and t4 == 0:
                        w0 += (t_ysmm_last, t_ad[2 + dn])
                    t_z = op("vector", lambda e, ps=PS[b], o=PS[6 + dn][:, 0:512],
                             sc=pstat[:, 34 + tg:35 + tg]:
                             e.tensor_scalar_mul(o, ps[:, 0:512], sc),
                             waits=w0, hard=(s2op[tg],))
                    ps_tok[b] = t_z
                    t_c0 = op("vector", lambda e, z=PS[6 + dn], o=cn:
                              e.tensor_single_scalar(o, z[:, 0:512], THS[0],
                                                     ALU.is_gt),
                              waits=(t_z, cnt_tok2[dn]))
                    t_c = t_c0
                    for th in THS[1:4]:
                        t_c = op("vector", lambda e, z=PS[6 + dn], o=cn, a2=cn,
                                 t_=th:
                                 e.scalar_tensor_tensor(o, z[:, 0:512], t_, a2,
                                                        ALU.is_gt, ALU.add),
                                 waits=(t_c,))
                    t_c4 = op("vector",
                              lambda e, z=PS[6 + dn],
                              o=cn.bitcast(BF16)[:, 0:512], a2=cn:
                              e.scalar_tensor_tensor(o, z[:, 0:512], THS[4], a2,
                                                     ALU.is_gt, ALU.add),
                              waits=(t_c,))
                    ps_tok[6 + dn] = t_c4
                    t_cd = dma("sync" if dn == 0 else "gpsimd",
                               sc_cnt.ap()[128 * tg:128 * (tg + 1),
                                           512 * dn:512 * (dn + 1)],
                               cn.bitcast(BF16)[:, 0:512], waits=(t_c4,))
                    cnt_tok2[dn] = t_cd
                    e_tok[dn] = t_cd

        t_wo = [dma("scalar", wout_sb[k],
                    wout_d.ap()[128 * k:128 * (k + 1), :],
                    waits=(e_last_mm1,) + tuple(t_woc)) for k in range(16)]

        t_ysb = dma("gpsimd", ysb[:], sc_ys.ap().partition_broadcast(128),
                    waits=tuple(t_ysd) + (e_last_mm1,))
        y2_tok_all = [None] * NCH
        y2_readers = [[None] * 16, [None] * 16]
        fu_tok = [(), ()]
        fsr_tok = [(), ()]
        ft9_tok = [None, None]
        fxn_tok = [None, None]
        fx_tok = [(), (), (), ()]
        cnt8 = [G[22][:].bitcast(BF16)[:, i * 1024:(i + 1) * 1024]
                for i in range(4)] + \
               [G[23][:].bitcast(BF16)[:, i * 1024:(i + 1) * 1024]
                for i in range(4)]
        cnt_readers = [None] * 8
        s_tok4 = [None] * 4
        s_tok2 = [None, None]
        g_tok2 = [None, None]
        sg_tok2 = [None, None]
        o_tok2 = [None, None]
        out_dmas = []

        def f_y2stage(fc):
            pb = fc % 2
            y2t = y2t2[pb]
            y2_tok = [None] * 16
            for bq in range(4):
                first = fc == 0 and bq < 2
                wu = (u_tok[0], u_tok[1], e_last_mm1) if first else fu_tok[bq % 2]
                ws = (sre_tok[0], e_last_mm1) if first else fsr_tok[bq % 2]
                t_lu = dma("sync", fub[bq % 2][:],
                           sc_u.ap()[:, 4 * bq:4 * bq + 4, CH * fc:CH * (fc + 1)],
                           waits=wu)
                t_ls = dma("scalar", fsrb[bq % 2][:],
                           sc_sres.ap()[:, 4 * bq:4 * bq + 4, CH * fc:CH * (fc + 1)],
                           waits=ws)
                t9s, y2s = [], []
                for j in range(4):
                    fo = 4 * bq + j
                    b = fo % 2
                    t_t9 = op("gpsimd", lambda e, o=f_t9[b],
                              i_=fub[bq % 2][:, 512 * j:512 * (j + 1)],
                              y=ysb[:, CH * fc:CH * (fc + 1)]:
                              e.tensor_tensor(o, i_, y, ALU.add),
                              waits=(t_lu, t_ysb, ft9_tok[b])
                              + ((e_tok[b], rl_tok[b], t_ysmm_last, t_ad[b],
                                  acc_tok[b], xcf_tok[b])
                                 if fc == 0 and fo < 2 else ()))
                    t_y2 = op("vector", lambda e, o=y2t[fo], a=f_t9[b],
                              s_=fsrb[bq % 2][:, 512 * j:512 * (j + 1)]:
                              e.tensor_tensor(o, a, s_, ALU.mult),
                              waits=(t_t9, t_ls, y2_readers[pb][fo])
                              + ((e_last_mm1,) if fc == 0 else ())
                              + ((e_last_mm2,) if fc == 1 else ()))
                    ft9_tok[b] = t_y2
                    y2_tok[fo] = t_y2
                    t9s.append(t_t9)
                    y2s.append(t_y2)
                fu_tok[bq % 2] = tuple(t9s)
                fsr_tok[bq % 2] = tuple(y2s)
            y2_tok_all[fc] = y2_tok

        t4_state = [None] * NCH

        def f_t4A(fc):
            pb = fc % 2
            y2t = y2t2[pb]
            y2_tok = y2_tok_all[fc]
            st = []
            for t4 in range(4):
                p = t4 % 2
                tg = 4 * fc + t4
                ci = t4 + 4 * (fc % 2)
                t_lx = dma("sync", f_x4[t4],
                           x_in.ap()[HALO + 128 * tg:HALO + 128 * (tg + 1), :],
                           waits=fx_tok[t4] if fx_tok[t4] else (e_last_mm2,))
                t_rec = op("vector", lambda e, o=f_xn[p], i_=f_x4[t4],
                           m=pstat[:, tg:tg + 1], r=pstat[:, 17 + tg:18 + tg]:
                           e.tensor_scalar(o, i_, m, r, ALU.subtract, ALU.mult),
                           waits=(t_lx,)
                           + ((fxn_tok[p],) if fxn_tok[p] else (e_last_mm2,)),
                           hard=(muop[tg], rop[tg]))
                t_lc = dma("scalar", cnt8[ci],
                           sc_cnt.ap()[128 * tg:128 * (tg + 1), :],
                           waits=(cnt_readers[ci], cnt_tok2[0], cnt_tok2[1])
                           + ((e_last_mm2,) if fc < 2 else ()))
                s_half = [None, None]
                mmw = [None, None]
                for k in range(16):
                    for dn in range(2):
                        b = 4 + 2 * (t4 % 2) + dn
                        w0 = (y2_tok[k], t_wo[k])
                        if k == 0:
                            w0 += (ps_tok[b],)
                        mmw[dn] = op("tensor",
                                     lambda e, ps=PS[b],
                                     l=y2t[k][:, 128 * t4:128 * (t4 + 1)],
                                     r=wout_sb[k][:, 512 * dn:512 * (dn + 1)],
                                     s=(k == 0), t=(k == 15):
                                     e.matmul(ps[:, 0:512], l, r, start=s, stop=t),
                                     waits=w0)
                for k in range(16):
                    y2_readers[pb][k] = mmw[1]
                for dn in range(2):
                    b = 4 + 2 * (t4 % 2) + dn
                    t_s = op("vector",
                             lambda e, ps=PS[b],
                             o=f_s4[t4][:, 512 * dn:512 * (dn + 1)],
                             x_=f_xn[p][:, 512 * dn:512 * (dn + 1)]:
                             e.tensor_tensor(o, ps[:, 0:512], x_, ALU.add),
                             waits=(mmw[dn], t_rec)
                             + ((s_tok4[t4],) if s_tok4[t4] else (e_last_mm2,)))
                    ps_tok[b] = t_s
                    s_half[dn] = t_s
                fxn_tok[p] = s_half[1]
                st.append((t4, p, tg, ci, t_lx, t_lc, t_rec, s_half))
            t4_state[fc] = st

        def f_t4B(fc):
            for (t4, p, tg, ci, t_lx, t_lc, t_rec, s_half) in t4_state[fc]:
                t_g = op("scalar", lambda e, i_=cnt8[ci], o=f_g2[p]:
                         e.activation(o, i_, AF.Silu, scale=c_p2),
                         waits=(t_lc,)
                         + ((g_tok2[p],) if g_tok2[p] else (e_last_mm2,)))
                cnt_readers[ci] = t_g
                t_sg = op("vector", lambda e, o=f_sg2[p], a=f_s4[t4], g=f_g2[p]:
                          e.tensor_tensor(o, a, g, ALU.mult),
                          waits=(t_g, s_half[0], s_half[1])
                          + ((sg_tok2[p],) if sg_tok2[p] else (e_last_mm2,)))
                s_tok4[t4] = t_sg
                g_tok2[p] = t_sg
                t_o = op("vector", lambda e, o=f_o2[p], a=f_sg2[p], x_=f_x4[t4]:
                         e.tensor_tensor(o, a, x_, ALU.add),
                         waits=(t_sg, t_lx)
                         + ((o_tok2[p],) if o_tok2[p] else (e_last_mm2,)))
                sg_tok2[p] = t_o
                t_od = dma("sync", out_t.ap()[128 * tg:128 * (tg + 1), :],
                           f_o2[p], waits=(t_o,))
                o_tok2[p] = t_od
                fx_tok[t4] = (t_o, t_rec)
                out_dmas.append(t_od)

        for stage in range(NCH + 1):
            if stage >= 1:
                f_t4A(stage - 1)
            if stage < NCH:
                f_y2stage(stage)
            if stage >= 1:
                f_t4B(stage - 1)

        if debug:
            tl = None
            for nm, src in (("u", sc_u), ("sres", sc_sres),
                            ("cnt_", sc_cnt), ("ys", sc_ys)):
                tl = dma("gpsimd", dbg[nm].ap(), src.ap(),
                         waits=tuple(out_dmas[-2:]) + ((tl,) if tl else ()))
            dma("gpsimd", dbg["delta"].ap(), sc_dl.ap(), waits=(out_dmas[-1],))
            dma("gpsimd", dbg["hgt"].ap(), hgT[:], waits=(t_hge, out_dmas[-1]))

        op("gpsimd", lambda e: e.engine_nop(), waits=tuple(out_dmas))

        S.finalize()
        with nc.Block() as block:
            block.sync(S.emitter("sync", sems))
            block.scalar(S.emitter("scalar", sems))
            block.vector(S.emitter("vector", sems))
            block.tensor(S.emitter("tensor", sems))
            block.gpsimd(S.emitter("gpsimd", sems))

    return nc


_NC_CACHE = {}


def _get_nc(debug=False):
    if debug not in _NC_CACHE:
        _NC_CACHE[debug] = _build(debug)
    return _NC_CACHE[debug]


def _prep_inputs(inputs):
    x = np.ascontiguousarray(inputs["L0_attn_x"], dtype=np.float32)
    B, T, _ = x.shape
    cps = 8 // B
    L = T // cps
    conv_w = np.ascontiguousarray(
        np.asarray(inputs["L0_attn_conv_w"], np.float32)
        .reshape(16, 128, 4).transpose(1, 0, 2).reshape(128, 64))
    conv_b = np.ascontiguousarray(
        np.asarray(inputs["L0_attn_conv_b"], np.float32).reshape(16, 128).T)
    d_r = np.ascontiguousarray(
        np.asarray(inputs["L0_attn_D"], np.float32).reshape(16, 128).T)
    common = dict(
        w_in=np.ascontiguousarray(inputs["L0_attn_W_in"], np.float32),
        w_dt=np.ascontiguousarray(inputs["L0_attn_W_dt"], np.float32),
        w_x=np.ascontiguousarray(inputs["L0_attn_W_x"], np.float32),
        w_out=np.ascontiguousarray(inputs["L0_attn_W_out"], np.float32),
        w_fc=np.ascontiguousarray(inputs["L0_ffn_W_fc"], np.float32),
        w_pr=np.ascontiguousarray(inputs["L0_ffn_W_proj"], np.float32),
        lngT=np.ascontiguousarray(
            np.asarray(inputs["L0_attn_ln_g"], np.float32).reshape(8, 128).T),
        lnbT=np.ascontiguousarray(
            np.asarray(inputs["L0_attn_ln_b"], np.float32).reshape(8, 128).T),
        convw=conv_w, convb=conv_b, dvec=d_r,
        bdt=np.ascontiguousarray(
            np.asarray(inputs["L0_attn_b_dt"], np.float32).reshape(16, 128).T),
        alog=np.ascontiguousarray(
            np.asarray(inputs["L0_attn_A_log"], np.float32).reshape(DS, 1)),
        identf=np.eye(128, dtype=np.float32),
    )
    in_maps = []
    for core in range(8):
        b, c = core // cps, core % cps
        t0 = c * L
        xs = np.zeros((HALO + L, D), np.float32)
        xs[HALO:] = x[b, t0:t0 + L]
        if c > 0:
            xs[:HALO] = x[b, t0 - HALO:t0]
        sel = np.zeros((DS, 8), np.float32)
        sel[:, core] = 1.0
        mask = np.full((128, 1), 0.0 if c == 0 else 1.0, np.float32)
        m = dict(common)
        m.update(x_in=np.ascontiguousarray(xs), selv=sel, maskv=mask)
        in_maps.append(m)
    return in_maps, (B, T, cps, L)


def kernel(**inputs):
    in_maps, (B, T, cps, L) = _prep_inputs(inputs)
    nc = _get_nc(False)
    res = run_bass_kernel_spmd(nc, in_maps, core_ids=list(range(8)))
    out = np.empty((B, T, D), np.float32)
    for core in range(8):
        b, c = core // cps, core % cps
        out[b, c * L:(c + 1) * L] = res.results[core]["out"]
    return out
